# version 01
# baseline (speedup 1.0000x reference)
import numpy as np
import ml_dtypes
from contextlib import ExitStack

import concourse.bass as bass
import concourse.mybir as mybir
from concourse.bass_utils import run_bass_kernel_spmd

F32 = mybir.dt.float32
BF16 = mybir.dt.bfloat16
AF = mybir.ActivationFunctionType
ALU = mybir.AluOpType
AX = mybir.AxisListType

D = 2048
HID = 8192
T = 2048
NCH = D // 128
TCH = T // 128
HCH = HID // 128
EPS = 1e-5
NEG = -1.0e30
ISQ_D = float(1.0 / np.sqrt(np.float64(D)))
ISQ_H = float(1.0 / np.sqrt(np.float64(HID)))
SM_SCALE = float(1.0 / np.sqrt(np.float64(128.0)))
MASKW = 6016
MASK_OFF = 1920


NOP16 = [True]


def _n16(t):
    if NOP16[0]:
        t.nop(cycle_cnt=16)


def _hadamard(n):
    h = np.array([[1.0]], dtype=np.float32)
    while h.shape[0] < n:
        h = np.block([[h, h], [h, -h]])
    return h


class Counter:

    def __init__(self, handle):
        self.h = handle
        self.n = 0
        self.max_waited = 0

    def inc(self, inst, k=1):
        if self.h is not None:
            inst.then_inc(self.h, k)
        self.n += k
        return self.n

    def wait(self, eng, val):
        assert val >= 0
        self.max_waited = max(self.max_waited, val)
        if val > 0 and self.h is not None:
            eng.wait_ge(self.h, val)

    def check(self, name=""):
        assert self.max_waited <= self.n, \
            f"sem {name}: waited {self.max_waited} > scheduled {self.n}"


class DSem:

    def __init__(self, handle):
        self.h = handle
        self.n = 0
        self.max_waited = 0

    def issue(self, inst):
        if self.h is not None:
            inst.then_inc(self.h, 16)
        self.n += 1
        return self.n

    def wait_k(self, eng, k):
        assert k >= 0
        self.max_waited = max(self.max_waited, k)
        if k > 0 and self.h is not None:
            eng.wait_ge(self.h, 16 * k)

    def check(self, name=""):
        assert self.max_waited <= self.n, \
            f"dsem {name}: waited {self.max_waited} > issued {self.n}"


_SER_OPS = {
    "tensor_copy", "tensor_tensor", "tensor_scalar", "tensor_scalar_mul",
    "tensor_scalar_add", "tensor_add", "tensor_mul", "tensor_sub", "tensor_max",
    "scalar_tensor_tensor", "tensor_reduce", "reduce_sum", "reduce_max",
    "reciprocal", "memset", "activation", "copy", "add", "mul", "square",
    "sqrt", "select", "stream_shuffle", "iota",
}


class SEng:

    def __init__(self, eng, ctr):
        self._e = eng
        self._c = ctr

    def __getattr__(self, name):
        f = getattr(self._e, name)
        if name not in _SER_OPS:
            return f

        def g(*a, **k):
            if self._c.n > 0:
                self._e.wait_ge(self._c.h, self._c.n)
            inst = f(*a, **k)
            self._c.inc(inst)
            return inst
        return g


class B:
    def __init__(self, nc, es):
        self.nc = nc
        self.es = es
        self.es0 = es
        self.block = None
        self._sems = []
        self.ser = {}

    def mk(self, se, ctr):
        se._e.wait_ge(se._c.h, se._c.n)
        ctr.inc(se._e.nop())

    def wrap(self, eng, key):
        if key not in self.ser:
            self.ser[key] = Counter(self.es0.enter_context(self.nc.semaphore("ser_" + key)))
            self._sems.append(("ser_" + key, self.ser[key]))
        return SEng(eng, self.ser[key])

    def sb(self, name, shape, dt):
        return self.es.enter_context(self.nc.sbuf_tensor(name, shape, dt))

    def ps(self, name, shape, dt=F32):
        return self.es.enter_context(self.nc.psum_tensor(name, shape, dt))

    def sem(self, name):
        c = Counter(self.es.enter_context(self.nc.semaphore(name)))
        self._sems.append((name, c))
        return c

    def dsem(self, name):
        c = DSem(self.es.enter_context(self.nc.semaphore(name)))
        self._sems.append((name, c))
        return c

    def check(self):
        for name, c in self._sems:
            c.check(name)



def emit_ln(b, x_dram, xn_sbuf, pe, dv, ac, scale_cols=None, affine=None,
            keep_ab=None, prefix="ln", out_t=T):
    nc, blk = b.nc, b.block
    xcol = b.sb(prefix + "_x", [128, 2 * NCH * 128], F32)
    sq = b.sb(prefix + "_sq", [128, NCH * 128], F32)
    srow = b.sb(prefix + "_sr", [1, 512], F32)
    arow = b.sb(prefix + "_ar", [1, 256], F32)
    onesc = b.sb(prefix + "_oc", [128, 1], F32)
    onesr = b.sb(prefix + "_or", [1, 128], F32)
    bc = b.sb(prefix + "_bc", [128, 256], F32)
    atmp = b.sb(prefix + "_at", [128, 128], F32)
    ps_s = b.ps(prefix + "_ps", [128, 512], F32)
    plc = b.sem(prefix + "_pl")
    lnl = [b.dsem(prefix + "_l0"), b.dsem(prefix + "_l1")]
    x3 = x_dram.rearrange("(c p) t -> p c t", p=128)

    pe0, dv0, ac0 = pe.n, dv.n, ac.n
    NT = TCH

    @blk.gpsimd
    def _(g):
        g = b.wrap(g, "pool")
        g.memset(onesc[:, :], 1.0)
        g.memset(onesr[:, :], 1.0)
        b.mk(g, plc)
    dv0 = dv.n

    @blk.sync
    def _(s):
        for tc in range(NT):
            if tc >= 2:
                ac.wait(s, ac0 + 3 * (tc - 2) + 1)
                pe.wait(s, pe0 + 2 * (tc - 2) + 1)
                dv.wait(s, dv0 + 3 * (tc - 2) + 3)
            lnl[tc % 2].issue(s.dma_start(
                out=xcol[:, (tc % 2) * NCH * 128:(tc % 2 + 1) * NCH * 128]
                .rearrange("p (c t) -> p c t", c=NCH),
                in_=x3[:, :, tc * 128:(tc + 1) * 128]))

    @blk.scalar
    def _(sc):
        sc = b.wrap(sc, "act")
        for tc in range(NT):
            lnl[tc % 2].wait_k(sc, tc // 2 + 1)
            if tc >= 1:
                pe.wait(sc, pe0 + 2 * (tc - 1) + 1)
            xc = xcol[:, (tc % 2) * NCH * 128:(tc % 2 + 1) * NCH * 128]
            sc.activation(sq[:, :], xc, AF.Square)
            b.mk(sc, ac)
            dv.wait(sc, dv0 + 3 * tc + 1)
            sc.activation(srow[0:1, 0:128], srow[0:1, 0:128], AF.Ln)
            sc.activation(arow[0:1, 0:128], srow[0:1, 0:128], AF.Exp, scale=-0.5)
            b.mk(sc, ac)
            ac.inc(sc._e.nop())

    @blk.tensor
    def _(t):
        for tc in range(NT):
            lnl[tc % 2].wait_k(t, tc // 2 + 1)
            if tc == 0:
                plc.wait(t, 1)
            if tc >= 1:
                dv.wait(t, dv0 + 3 * (tc - 1) + 3)
            xc = xcol[:, (tc % 2) * NCH * 128:(tc % 2 + 1) * NCH * 128]
            for c in range(NCH):
                t.matmul(ps_s[0:1, 0:128], onesc[:, :], xc[:, c * 128:(c + 1) * 128],
                         start=(c == 0), stop=(c == NCH - 1))
            _n16(t)
            ac.wait(t, ac0 + 3 * tc + 1)
            inst = None
            for c in range(NCH):
                inst = t.matmul(ps_s[0:1, 128:256], onesc[:, :], sq[:, c * 128:(c + 1) * 128],
                                start=(c == 0), stop=(c == NCH - 1))
            pe.inc(inst)
            _n16(t)
            dv.wait(t, dv0 + 3 * tc + 2)
            t.matmul(ps_s[:, 256:384], onesr[0:1, :], arow[0:1, 0:128], start=True, stop=True)
            _n16(t)
            inst = t.matmul(ps_s[:, 384:512], onesr[0:1, :], arow[0:1, 128:256],
                            start=True, stop=True)
            pe.inc(inst)
            _n16(t)

    @blk.vector
    def _(v):
        v = b.wrap(v, "dve")
        for tc in range(NT):
            pe.wait(v, pe0 + 2 * tc + 1)
            v.tensor_scalar(srow[0:1, 256:384], ps_s[0:1, 0:128], 1.0 / D, None, ALU.mult)
            v.tensor_scalar(srow[0:1, 384:512], ps_s[0:1, 128:256], 1.0 / D, None, ALU.mult)
            v.tensor_tensor(srow[0:1, 0:128], srow[0:1, 256:384], srow[0:1, 256:384], ALU.mult)
            v.scalar_tensor_tensor(srow[0:1, 0:128], srow[0:1, 384:512], EPS,
                                   srow[0:1, 0:128], ALU.add, ALU.subtract)
            b.mk(v, dv)
            ac.wait(v, ac0 + 3 * tc + 2)
            v.scalar_tensor_tensor(arow[0:1, 128:256], srow[0:1, 256:384], -1.0,
                                   arow[0:1, 0:128], ALU.mult, ALU.mult)
            b.mk(v, dv)
            pe.wait(v, pe0 + 2 * tc + 2)
            v.tensor_copy(bc[:, 0:128], ps_s[:, 256:384])
            v.tensor_copy(bc[:, 128:256], ps_s[:, 384:512])
            xc = xcol[:, (tc % 2) * NCH * 128:(tc % 2 + 1) * NCH * 128]
            for c in range(NCH):
                dst = xn_sbuf[:, c * out_t + tc * 128: c * out_t + (tc + 1) * 128]
                tmp = atmp[:, :]
                v.tensor_tensor(tmp, xc[:, c * 128:(c + 1) * 128], bc[:, 0:128], ALU.mult)
                if affine is not None:
                    gam, bet = affine
                    v.tensor_tensor(tmp, tmp, bc[:, 128:256], ALU.add)
                    v.tensor_scalar(tmp, tmp, gam[:, c:c + 1], None, ALU.mult)
                    v.tensor_scalar(dst, tmp, bet[:, c:c + 1], None, ALU.add)
                elif scale_cols is not None:
                    v.tensor_tensor(tmp, tmp, bc[:, 128:256], ALU.add)
                    v.tensor_scalar(dst, tmp, scale_cols[:, c:c + 1], None, ALU.mult)
                else:
                    v.tensor_tensor(dst, tmp, bc[:, 128:256], ALU.add)
            if keep_ab is not None:
                a_all, b_all = keep_ab
                v.tensor_copy(a_all[:, tc * 128:(tc + 1) * 128], bc[:, 0:128])
                v.tensor_copy(b_all[:, tc * 128:(tc + 1) * 128], bc[:, 128:256])
            b.mk(v, dv)

    return dv.n



def emit_dance(b, hbuf, ident, kr64, banks, stg, pe, dv, ac, ready_dv):
    blk = b.block
    NP, G = 256, 8
    NG = NP // G
    h3 = hbuf[:, :].rearrange("p (c t) -> p c t", c=HCH)

    def pv(j):
        return h3[:, :, 2 * j:2 * j + 2]

    def reg(g, k):
        kk = (g % 2) * 8 + k
        return banks[kk // 4][:, (kk % 4) * 128:(kk % 4) * 128 + 128]

    def slot(stage, g, k):
        kk = (g % 2) * 8 + k
        return stg[:, (stage * 16 + kk) * 128:(stage * 16 + kk) * 128 + 128]

    pe0, dv0, ac0 = pe.n, dv.n, ac.n

    @blk.vector
    def _(v):
        v = b.wrap(v, "dve")
        for g in range(NG):
            if g == 0:
                dv.wait(v, ready_dv)
            if g >= 2:
                pe.wait(v, pe0 + 3 * (g - 2) + 1)
            for k in range(G):
                v.tensor_copy(slot(0, g, k), pv(g * G + k))
            b.mk(v, dv)
            pe.wait(v, pe0 + 3 * g + 2)
            if g >= 2:
                pe.wait(v, pe0 + 3 * (g - 2) + 3)
            for k in range(G):
                v.tensor_copy(slot(2, g, k), reg(g, k))
            b.mk(v, dv)

    @blk.tensor
    def _(t):
        for g in range(NG):
            dv.wait(t, dv0 + 2 * g + 1)
            if g >= 2:
                ac.wait(t, ac0 + 2 * (g - 2) + 2)
            inst = None
            for k in range(G):
                inst = t.matmul(reg(g, k), slot(0, g, k), ident[:, :], start=True, stop=True)
                if k < G - 1:
                    _n16(t)
            pe.inc(inst)
            _n16(t)
            ac.wait(t, ac0 + 2 * g + 1)
            for k in range(G):
                inst = t.matmul(reg(g, k), kr64[:, :], slot(1, g, k), start=True, stop=True)
                if k < G - 1:
                    _n16(t)
            pe.inc(inst)
            _n16(t)
            dv.wait(t, dv0 + 2 * g + 2)
            for k in range(G):
                inst = t.matmul(reg(g, k), slot(2, g, k), ident[:, :], start=True, stop=True)
                if k < G - 1:
                    _n16(t)
            pe.inc(inst)
            _n16(t)

    @blk.scalar
    def _(sc):
        sc = b.wrap(sc, "act")
        for g in range(NG):
            pe.wait(sc, pe0 + 3 * g + 1)
            if g >= 2:
                pe.wait(sc, pe0 + 3 * (g - 2) + 2)
            for k in range(G):
                sc.copy(slot(1, g, k), reg(g, k))
            b.mk(sc, ac)
            pe.wait(sc, pe0 + 3 * g + 3)
            for k in range(G):
                sc.copy(pv(g * G + k), reg(g, k))
            b.mk(sc, ac)

    return dv.n, ac.n



def build_ffn(variant, stop_after=None, csteps=None):
    assert variant in ("l0", "l1")
    assert stop_after in (None, "ln", "hx", "spill", "cpart")
    nc = bass.Bass("TRN2", target_bir_lowering=False, debug=False)
    x_in = nc.dram_tensor("x_in", [D, T], F32, kind="ExternalInput").ap()
    x_out = nc.dram_tensor("x_out", [D, T], F32, kind="ExternalOutput").ap()
    ident_i = nc.dram_tensor("ident", [128, 128], BF16, kind="ExternalInput").ap()
    h128_i = nc.dram_tensor("h128", [128, 128], BF16, kind="ExternalInput").ap()
    kr64_i = nc.dram_tensor("kr64", [128, 128], BF16, kind="ExternalInput").ap()
    w_up = nc.dram_tensor("w_up", [HID, D], F32, kind="ExternalInput").ap()
    w_gate = nc.dram_tensor("w_gate", [HID, D], F32, kind="ExternalInput").ap()
    w_down = nc.dram_tensor("w_down", [D, HID], F32, kind="ExternalInput").ap()
    if variant == "l0":
        h2048_i = nc.dram_tensor("h2048", [D, D], BF16, kind="ExternalInput").ap()
        su_ug_i = nc.dram_tensor("su_ug", [128, NCH], F32, kind="ExternalInput").ap()
        su_dp_i = nc.dram_tensor("su_dp", [128, HCH], F32, kind="ExternalInput").ap()
        wsc_up_i = nc.dram_tensor("wsc_up", [128, HCH], F32, kind="ExternalInput").ap()
        wsc_gt_i = nc.dram_tensor("wsc_gt", [128, HCH], F32, kind="ExternalInput").ap()
        wsc_dp_i = nc.dram_tensor("wsc_dp", [128, NCH], F32, kind="ExternalInput").ap()
    w_up_b = nc.dram_tensor("w_up_b", [HID, D], BF16, kind="Internal").ap()
    w_gate_b = nc.dram_tensor("w_gate_b", [HID, D], BF16, kind="Internal").ap()
    w_down_b = nc.dram_tensor("w_down_b", [D, HID], BF16, kind="Internal").ap()
    w_downT = nc.dram_tensor("w_downT", [HID, D], BF16, kind="Internal").ap()
    za_d = nc.dram_tensor("za_d", [HCH, 128, T], BF16, kind="Internal").ap()

    es = ExitStack()
    b = B(nc, es)
    ident = b.sb("identt", [128, 128], BF16)
    h128 = b.sb("h128t", [128, 128], BF16)
    kr64 = b.sb("kr64t", [128, 128], BF16)
    grow = b.sb("grow", [128, 2 * T], BF16)
    sg = b.sb("sg", [128, 256], F32)
    wtt = b.sb("wtt", [128, 2 * D], BF16)
    xres = b.sb("xres", [128, 1024], F32)
    oute = b.sb("oute", [128, 1024], F32)
    if variant == "l0":
        svec = b.sb("svec", [128, 64], F32)
        gvec = b.sb("gvec", [128, 64], F32)
        dvec = b.sb("dvec", [128, 16], F32)
        vtmp = b.sb("vtmp", [128, 80], F32)

    ps_u = b.ps("ps_u", [128, 512], F32)
    ps_g = b.ps("ps_g", [128, 512], F32)
    ps_a = b.ps("ps_a", [128, 512], F32)
    ps_b = b.ps("ps_b", [128, 512], F32)
    ps_c = b.ps("ps_c", [128, 512], F32)
    ps_d = b.ps("ps_d", [128, 512], F32)

    es_xa = ExitStack()
    b.es = es_xa
    xa = b.sb("xa", [128, NCH * T], BF16)
    b.es = es

    with nc.Block() as blk:
        b.block = blk
        pe = b.sem("pe")
        dv = b.sem("dv")
        ac = b.sem("ac")
        ldc = b.dsem("ldc")
        csem = b.dsem("csem")
        spl = [b.dsem("spl0"), b.dsem("spl1")]
        sto = [b.dsem("sto0"), b.dsem("sto1")]

        @blk.gpsimd
        def _(g):
            g = b.wrap(g, "pool")
            csem.issue(g.dma_start(out=w_up_b[:, :], in_=w_up[:, :]))
            csem.issue(g.dma_start(out=w_gate_b[:, :], in_=w_gate[:, :]))
            csem.issue(g.dma_start(out=w_down_b[:, :], in_=w_down[:, :]))
        NCAST = 3

        @blk.sync
        def _(s):
            ldc.issue(s.dma_start(out=ident[:, :], in_=ident_i[:, :]))
            ldc.issue(s.dma_start(out=h128[:, :], in_=h128_i[:, :]))
            ldc.issue(s.dma_start(out=kr64[:, :], in_=kr64_i[:, :]))
            if variant == "l0":
                ldc.issue(s.dma_start(out=svec[:, 0:64], in_=wsc_up_i[:, :]))
                ldc.issue(s.dma_start(out=gvec[:, 0:64], in_=wsc_gt_i[:, :]))
                ldc.issue(s.dma_start(out=dvec[:, 0:16], in_=wsc_dp_i[:, :]))
                ldc.issue(s.dma_start(out=vtmp[:, 0:64], in_=su_dp_i[:, :]))
                ldc.issue(s.dma_start(out=vtmp[:, 64:80], in_=su_ug_i[:, :]))
        NLDC = ldc.n

        @blk.vector
        def _(v):
            v = b.wrap(v, "dve")
            ldc.wait_k(v, NLDC)
            if variant == "l0":
                v.tensor_scalar(gvec[:, 0:64], gvec[:, 0:64], ISQ_D, None, ALU.mult)
                v.tensor_scalar(svec[:, 0:64], svec[:, 0:64], ISQ_D, None, ALU.mult)
                v.tensor_tensor(svec[:, 0:64], svec[:, 0:64], vtmp[:, 0:64], ALU.mult)
                v.tensor_scalar(dvec[:, 0:16], dvec[:, 0:16], 0.5 * ISQ_H, None, ALU.mult)
            b.mk(v, dv)
        vecs_dv = dv.n

        es_ln = ExitStack()
        b.es = es_ln
        sc_cols = vtmp[:, 64:80] if variant == "l0" else None
        ln_done = emit_ln(b, x_in, xa, pe, dv, ac, scale_cols=sc_cols, prefix="lna")
        es_ln.close()
        b.es = es

        def finish(stacks):
            b.check()
            for st_ in stacks:
                st_.close()
            es.close()

        if stop_after == "ln":
            @blk.gpsimd
            def _(g):
                g = b.wrap(g, "pool")
                dv.wait(g, ln_done)
                csem.issue(g.dma_start(
                    out=x_out.rearrange("(c p) t -> p c t", p=128),
                    in_=xa[:, :].rearrange("p (c t) -> p c t", c=NCH)))
                g.wait_ge(csem.h, 16 * csem.n)
            finish([es_xa])
            return nc

        src = xa
        if variant == "l0":
            es_b = ExitStack()
            b.es = es_b
            x1 = b.sb("x1", [128, NCH * T], BF16)
            hblk = b.sb("hblk", [128, 2 * NCH * 128], BF16)
            b.es = es
            hbl = [b.dsem("hbl0"), b.dsem("hbl1")]
            h3 = h2048_i.rearrange("(c p) o -> p c o", p=128)
            peB0, dvB0 = pe.n, dv.n

            @blk.sync
            def _(s):
                for oc in range(NCH):
                    if oc >= 2:
                        pe.wait(s, peB0 + 16 * (oc - 1))
                    hbl[oc % 2].issue(s.dma_start(
                        out=hblk[:, (oc % 2) * NCH * 128:(oc % 2 + 1) * NCH * 128]
                        .rearrange("p (c o) -> p c o", c=NCH),
                        in_=h3[:, :, oc * 128:(oc + 1) * 128]))

            @blk.tensor
            def _(t):
                dv.wait(t, ln_done)
                pse = [ps_a, ps_b, ps_c, ps_d]
                for oc in range(NCH):
                    hbl[oc % 2].wait_k(t, oc // 2 + 1)
                    hb = hblk[:, (oc % 2) * NCH * 128:(oc % 2 + 1) * NCH * 128]
                    for tc in range(TCH):
                        j = oc * TCH + tc
                        if j >= 4:
                            dv.wait(t, dvB0 + (j - 4) + 1)
                        psx = pse[j % 4]
                        inst = None
                        for c in range(NCH):
                            inst = t.matmul(psx[:, 0:128], hb[:, c * 128:(c + 1) * 128],
                                            xa[:, c * T + tc * 128: c * T + (tc + 1) * 128],
                                            start=(c == 0), stop=(c == NCH - 1))
                        pe.inc(inst)
                        _n16(t)

            @blk.vector
            def _(v):
                v = b.wrap(v, "dve")
                pse = [ps_a, ps_b, ps_c, ps_d]
                for j in range(NCH * TCH):
                    pe.wait(v, peB0 + j + 1)
                    oc, tc = divmod(j, TCH)
                    v.tensor_copy(
                        x1[:, oc * T + tc * 128: oc * T + (tc + 1) * 128],
                        pse[j % 4][:, 0:128])
                    b.mk(v, dv)
            src = x1
        phaseB_dv = dv.n

        if stop_after == "hx":
            @blk.gpsimd
            def _(g):
                g = b.wrap(g, "pool")
                dv.wait(g, phaseB_dv)
                csem.issue(g.dma_start(
                    out=x_out.rearrange("(c p) t -> p c t", p=128),
                    in_=src[:, :].rearrange("p (c t) -> p c t", c=NCH)))
                g.wait_ge(csem.h, 16 * csem.n)
            finish([es_b, es_xa] if variant == "l0" else [es_xa])
            return nc

        es_w = ExitStack()
        b.es = es_w
        HBS = 256
        CPB = HBS // 128
        NHB = HID // HBS
        wblk = b.sb("wblk", [128, 2 * NCH * HBS], BF16)
        wblk2 = b.sb("wblk2", [128, 2 * NCH * HBS], BF16)
        b.es = es
        wbl = [b.dsem("wbl0"), b.dsem("wbl1")]

        peC0, dvC0, acC0 = pe.n, dv.n, ac.n
        wbufs = [wblk, wblk2]
        NSTEP = NHB * CPB * TCH
        if stop_after == "cpart":
            NSTEP = csteps
        NHB_USED = NHB if stop_after != "cpart" else min(
            NHB, (NSTEP + CPB * TCH - 1) // (CPB * TCH))

        @blk.sync
        def _(s):
            csem.wait_k(s, NCAST)
            for hb in range(NHB_USED):
                if hb < 2:
                    dv.wait(s, phaseB_dv)
                if hb >= 2:
                    pe.wait(s, peC0 + 3 * ((hb - 1) * CPB * TCH - 1) + 3)
                wb_ = wbufs[hb % 2]
                for c in range(NCH):
                    wbl[hb % 2].issue(s.dma_start_transpose(
                        wb_[:, c * HBS:(c + 1) * HBS],
                        w_up_b[hb * HBS:(hb + 1) * HBS, c * 128:(c + 1) * 128]))
                    wbl[hb % 2].issue(s.dma_start_transpose(
                        wb_[:, NCH * HBS + c * HBS:NCH * HBS + (c + 1) * HBS],
                        w_gate_b[hb * HBS:(hb + 1) * HBS, c * 128:(c + 1) * 128]))

        @blk.tensor
        def _(t):
            dv.wait(t, phaseB_dv)
            dv.wait(t, vecs_dv)
            for i in range(NSTEP):
                hb, r = divmod(i, CPB * TCH)
                hc4, tc = divmod(r, TCH)
                wb_ = wbufs[hb % 2]
                if r == 0:
                    wbl[hb % 2].wait_k(t, 2 * NCH * (hb // 2 + 1))
                if i >= 2:
                    ac.wait(t, acC0 + (i - 2) + 1)
                    dv.wait(t, dvC0 + 2 * (i - 2) + 1)
                psg = [ps_g, ps_b][i % 2]
                psu = [ps_u, ps_c][i % 2]
                inst = None
                for c in range(NCH):
                    inst = t.matmul(psg[:, 0:128],
                                    wb_[:, NCH * HBS + c * HBS + hc4 * 128:
                                        NCH * HBS + c * HBS + hc4 * 128 + 128],
                                    src[:, c * T + tc * 128: c * T + (tc + 1) * 128],
                                    start=(c == 0), stop=(c == NCH - 1))
                pe.inc(inst)
                _n16(t)
                for c in range(NCH):
                    inst = t.matmul(psu[:, 0:128],
                                    wb_[:, c * HBS + hc4 * 128: c * HBS + hc4 * 128 + 128],
                                    src[:, c * T + tc * 128: c * T + (tc + 1) * 128],
                                    start=(c == 0), stop=(c == NCH - 1))
                pe.inc(inst)
                _n16(t)
                if i >= 1:
                    ip = i - 1
                    hcp, tcp = divmod(ip, TCH)
                    dv.wait(t, dvC0 + 2 * ip + 1)
                    if ip >= 2:
                        dv.wait(t, dvC0 + 2 * (ip - 2) + 2)
                    psa = [ps_a, ps_d][ip % 2]
                    inst = t.matmul(psa[:, 0:128], h128[:, :],
                                    grow[:, (hcp % 2) * T + tcp * 128:(hcp % 2) * T + (tcp + 1) * 128],
                                    start=True, stop=True)
                    pe.inc(inst)
                    _n16(t)
                else:
                    pe.inc(t.drain())
            ip = NSTEP - 1
            dv.wait(t, dvC0 + 2 * ip + 1)
            if ip >= 2:
                dv.wait(t, dvC0 + 2 * (ip - 2) + 2)
            psa = [ps_a, ps_d][ip % 2]
            inst = t.matmul(psa[:, 0:128], h128[:, :],
                            grow[:, ((ip // TCH) % 2) * T + (ip % TCH) * 128:
                                 ((ip // TCH) % 2) * T + (ip % TCH + 1) * 128],
                            start=True, stop=True)
            pe.inc(inst)
            _n16(t)

        def peC_up(i):
            return peC0 + 3 * i + 2

        def peC_stA(i):
            return peC0 + 3 * (i + 1) + 3 if i < NSTEP - 1 else peC0 + 3 * NSTEP + 1

        @blk.scalar
        def _(sc):
            sc = b.wrap(sc, "act")
            for i in range(NSTEP):
                hc = i // TCH
                pe.wait(sc, peC_up(i))
                if i >= 2:
                    dv.wait(sc, dvC0 + 2 * (i - 2) + 1)
                sgs = sg[:, (i % 2) * 128:(i % 2) * 128 + 128]
                psg = [ps_g, ps_b][i % 2]
                if variant == "l0":
                    sc.activation(sgs, psg[:, 0:128], AF.Silu,
                                  scale=gvec[:, hc:hc + 1])
                else:
                    sc.activation(sgs, psg[:, 0:128], AF.Silu)
                b.mk(sc, ac)

        @blk.vector
        def _(v):
            v = b.wrap(v, "dve")
            for i in range(NSTEP):
                hc, tc = divmod(i, TCH)
                ac.wait(v, acC0 + i + 1)
                if hc >= 2 and tc == 0 and stop_after != "cpart":
                    spl[hc % 2].wait_k(v, hc // 2)
                dst = grow[:, (hc % 2) * T + tc * 128:(hc % 2) * T + (tc + 1) * 128]
                sgs = sg[:, (i % 2) * 128:(i % 2) * 128 + 128]
                psu = [ps_u, ps_c][i % 2]
                if variant == "l0":
                    v.scalar_tensor_tensor(dst, psu[:, 0:128], svec[:, hc:hc + 1],
                                           sgs, ALU.mult, ALU.mult)
                else:
                    v.tensor_tensor(dst, psu[:, 0:128], sgs, ALU.mult)
                b.mk(v, dv)
                pe.wait(v, peC_stA(i))
                psa = [ps_a, ps_d][i % 2]
                v.tensor_copy(dst, psa[:, 0:128])
                b.mk(v, dv)

        if stop_after == "cpart":
            @blk.gpsimd
            def _(g):
                g = b.wrap(g, "pool")
                dv.wait(g, dvC0 + 2 * NSTEP)
                csem.issue(g.dma_start(out=x_out[0:128, :], in_=grow[:, 0:T]))
                g.wait_ge(csem.h, 16 * csem.n)
            finish([es_w, es_b, es_xa] if variant == "l0" else [es_w, es_xa])
            return nc

        @blk.gpsimd
        def _(g):
            g = b.wrap(g, "pool")
            for hc in range(HCH):
                i_last = hc * TCH + (TCH - 1)
                dv.wait(g, dvC0 + 2 * i_last + 2)
                spl[hc % 2].issue(g.dma_start(
                    out=za_d[hc, :, :], in_=grow[:, (hc % 2) * T:(hc % 2) * T + T]))
        NSPL = (spl[0].n, spl[1].n)

        if stop_after == "spill":
            @blk.gpsimd
            def _(g):
                g = b.wrap(g, "pool")
                spl[0].wait_k(g, NSPL[0])
                spl[1].wait_k(g, NSPL[1])
                csem.issue(g.dma_start(out=x_out.rearrange("(c p) t -> c p t", p=128),
                                       in_=za_d[0:NCH, :, :]))
                g.wait_ge(csem.h, 16 * csem.n)
            finish([es_w, es_b, es_xa] if variant == "l0" else [es_w, es_xa])
            return nc

        wtl = [b.dsem("wtl0"), b.dsem("wtl1")]
        wts = [b.dsem("wts0"), b.dsem("wts1")]

        @blk.sync
        def _(s):
            csem.wait_k(s, NCAST)
            for c in range(HCH):
                if c >= 2:
                    wts[c % 2].wait_k(s, c // 2)
                wtl[c % 2].issue(s.dma_start_transpose(
                    wtt[:, (c % 2) * D:(c % 2) * D + D],
                    w_down_b[0:D, c * 128:(c + 1) * 128]))

        @blk.scalar
        def _(sc):
            sc = b.wrap(sc, "act")
            for c in range(HCH):
                wtl[c % 2].wait_k(sc, c // 2 + 1)
                wts[c % 2].issue(sc.dma_start(out=w_downT[c * 128:(c + 1) * 128, :],
                                              in_=wtt[:, (c % 2) * D:(c % 2) * D + D]))
        NWTS = (wts[0].n, wts[1].n)

        es_w.close()
        if variant == "l0":
            es_b.close()
        es_xa.close()
        es_d = ExitStack()
        b.es = es_d
        hbuf = b.sb("hbuf", [128, HCH * 512], BF16)
        wdn = b.sb("wdn", [128, HCH * 128], BF16)
        wdn2 = b.sb("wdn2", [128, HCH * 128], BF16)
        dstg = b.sb("dstg", [128, 48 * 128], BF16)
        b.es = es

        hbs = b.dsem("hbs")
        wdl = [b.dsem("wdl0"), b.dsem("wdl1")]
        xrl = [b.dsem("xrl0"), b.dsem("xrl1")]

        wdt3 = w_downT.rearrange("(c p) o -> p c o", p=128)
        wdbufs = [wdn, wdn2]
        last_down_pe = [None]
        prev_dv_end = [None]
        for tt in range(4):
            @blk.sync
            def _(s, tt=tt):
                spl[0].wait_k(s, NSPL[0])
                spl[1].wait_k(s, NSPL[1])
                if tt > 0:
                    pe.wait(s, last_down_pe[0])
                hbs.issue(s.dma_start(
                    out=hbuf[:, :].rearrange("p (c t) -> p c t", c=HCH),
                    in_=za_d[:, :, tt * 512:(tt + 1) * 512].rearrange("c p t -> p c t")))

            @blk.vector
            def _(v, tt=tt):
                v = b.wrap(v, "dve")
                hbs.wait_k(v._e, tt + 1)
                dv.inc(v._e.nop())
            hb_dv = dv.n

            dance_dv, dance_ac = emit_dance(b, hbuf, ident, kr64,
                                            [ps_a, ps_b, ps_c, ps_u], dstg,
                                            pe, dv, ac, hb_dv)

            peD0 = pe.n
            dvD0 = dv.n

            @blk.sync
            def _(s, tt=tt, peD0=peD0, dvD0=dvD0):
                wts[0].wait_k(s, NWTS[0])
                wts[1].wait_k(s, NWTS[1])
                if tt > 0:
                    pe.wait(s, last_down_pe[0])
                    dv.wait(s, prev_dv_end[0])
                for oc in range(NCH):
                    if oc >= 2:
                        pe.wait(s, peD0 + (oc - 1))
                        dv.wait(s, dvD0 + (oc - 1))
                    wdl[oc % 2].issue(s.dma_start(
                        out=wdbufs[oc % 2][:, :].rearrange("p (c o) -> p c o", c=HCH),
                        in_=wdt3[:, :, oc * 128:(oc + 1) * 128]))
                    xrl[oc % 2].issue(s.dma_start(
                        out=xres[:, (oc % 2) * 512:(oc % 2) * 512 + 512],
                        in_=x_in[oc * 128:(oc + 1) * 128, tt * 512:(tt + 1) * 512]))

            wd_base = (wdl[0].n - NCH // 2, wdl[1].n - NCH // 2)
            xr_base = (xrl[0].n - NCH // 2, xrl[1].n - NCH // 2)

            @blk.tensor
            def _(t, tt=tt, dance_dv=dance_dv, dance_ac=dance_ac,
                  peD0=peD0, dvD0=dvD0, wd_base=wd_base):
                dv.wait(t, dance_dv)
                ac.wait(t, dance_ac)
                for oc in range(NCH):
                    wdl[oc % 2].wait_k(t, wd_base[oc % 2] + oc // 2 + 1)
                    if oc >= 1:
                        dv.wait(t, dvD0 + oc)
                    wb_ = wdbufs[oc % 2]
                    inst = None
                    for tc4 in range(4):
                        for c in range(HCH):
                            inst = t.matmul(
                                ps_d[:, tc4 * 128:(tc4 + 1) * 128],
                                wb_[:, c * 128:(c + 1) * 128],
                                hbuf[:, c * 512 + tc4 * 128: c * 512 + tc4 * 128 + 128],
                                start=(c == 0), stop=(c == HCH - 1))
                        _n16(t)
                    pe.inc(inst)
            last_down_pe[0] = peD0 + NCH

            sto_base = (sto[0].n, sto[1].n)

            @blk.vector
            def _(v, tt=tt, peD0=peD0, xr_base=xr_base, sto_base=sto_base):
                v = b.wrap(v, "dve")
                for oc in range(NCH):
                    pe.wait(v, peD0 + oc + 1)
                    xrl[oc % 2].wait_k(v, xr_base[oc % 2] + oc // 2 + 1)
                    if oc >= 2:
                        sto[oc % 2].wait_k(v, sto_base[oc % 2] + oc // 2)
                    elif tt > 0:
                        sto[oc % 2].wait_k(v, sto_base[oc % 2])
                    outs = oute[:, (oc % 2) * 512:(oc % 2) * 512 + 512]
                    xrs = xres[:, (oc % 2) * 512:(oc % 2) * 512 + 512]
                    if variant == "l0":
                        v.scalar_tensor_tensor(outs, ps_d[:, :], dvec[:, oc:oc + 1],
                                               xrs, ALU.mult, ALU.add)
                    else:
                        v.scalar_tensor_tensor(outs, ps_d[:, :], 0.5 * ISQ_H,
                                               xrs, ALU.mult, ALU.add)
                    b.mk(v, dv)

            @blk.scalar
            def _(sc, tt=tt, dvD0=dvD0):
                sc = b.wrap(sc, "act")
                for oc in range(NCH):
                    dv.wait(sc, dvD0 + oc + 1)
                    sto[oc % 2].issue(sc.dma_start(
                        out=x_out[oc * 128:(oc + 1) * 128, tt * 512:(tt + 1) * 512],
                        in_=oute[:, (oc % 2) * 512:(oc % 2) * 512 + 512]))
            prev_dv_end[0] = dvD0 + NCH

        @blk.sync
        def _(s):
            sto[0].wait_k(s, sto[0].n)
            sto[1].wait_k(s, sto[1].n)
        es_d.close()
        b.check()

    es.close()
    return nc




class _MockInst:
    def then_inc(self, sem, k=1):
        return self


class MockEng:

    def __getattr__(self, name):
        def f(*a, **k):
            return _MockInst()
        return f


class Marks:

    def __init__(self):
        self.d = {}
        self.plan = True

    def set(self, key, val):
        if self.plan:
            self.d[key] = val
        else:
            assert self.d[key] == val, f"mark {key}: plan {self.d[key]} != emit {val}"

    def get(self, key):
        if self.plan and key not in self.d:
            return 0
        return self.d[key]


def _plan_counters(names):
    return {n: Counter(None) for n in names}


def _plan_dsems(names):
    return {n: DSem(None) for n in names}



def build_attn0_qkv():
    nc = bass.Bass("TRN2", target_bir_lowering=False, debug=False)
    x_in = nc.dram_tensor("x_in", [D, T], F32, kind="ExternalInput").ap()
    wq = nc.dram_tensor("wq", [D, D], F32, kind="ExternalInput").ap()
    wk = nc.dram_tensor("wk", [D, D], F32, kind="ExternalInput").ap()
    wv = nc.dram_tensor("wv", [D, D], F32, kind="ExternalInput").ap()
    q_t = nc.dram_tensor("q_t", [NCH, TCH, 128, 128], BF16, kind="ExternalOutput").ap()
    k_t = nc.dram_tensor("k_t", [NCH, TCH, 128, 128], BF16, kind="ExternalOutput").ap()
    v_t = nc.dram_tensor("v_t", [TCH, NCH, 128, 128], BF16, kind="ExternalOutput").ap()
    wq_b = nc.dram_tensor("wq_b", [D, D], BF16, kind="Internal").ap()
    wk_b = nc.dram_tensor("wk_b", [D, D], BF16, kind="Internal").ap()
    wv_b = nc.dram_tensor("wv_b", [D, D], BF16, kind="Internal").ap()

    es = ExitStack()
    b = B(nc, es)
    wt0 = b.sb("wt0", [128, NCH * D], BF16)
    ost = b.sb("ost", [128, 2 * TCH * 128], BF16)
    psl = [b.ps(f"ps{q}", [128, 512], F32) for q in range(1, 5)]
    es_xa = ExitStack()
    b.es = es_xa
    xa = b.sb("xa", [128, NCH * T], BF16)
    b.es = es

    with nc.Block() as blk:
        b.block = blk
        pe = b.sem("pe")
        dv = b.sem("dv")
        ac = b.sem("ac")
        csem = b.dsem("csem")
        wls = [b.dsem("wl0"), b.dsem("wl1")]
        sto = [b.dsem("sto0"), b.dsem("sto1")]

        @blk.gpsimd
        def _(g):
            g = b.wrap(g, "pool")
            csem.issue(g.dma_start(out=wq_b[:, :], in_=wq[:, :]))
            csem.issue(g.dma_start(out=wk_b[:, :], in_=wk[:, :]))
            csem.issue(g.dma_start(out=wv_b[:, :], in_=wv[:, :]))
        NCAST = 3

        es_ln = ExitStack()
        b.es = es_ln
        ln_done = emit_ln(b, x_in, xa, pe, dv, ac, prefix="lnq")
        es_ln.close()
        b.es = es

        srcs = [wq_b, wk_b, wv_b]
        outs = [q_t, k_t, v_t]
        peM0 = pe.n
        dvM0 = dv.n

        @blk.sync
        def _(s):
            csem.wait_k(s, NCAST)
            for m in range(3):
                if m >= 1:
                    pe.wait(s, peM0 + m * 256)
                for c in range(NCH):
                    wls[0].issue(s.dma_start_transpose(
                        wt0[:, c * D:(c + 1) * D],
                        srcs[m][0:D, c * 128:(c + 1) * 128]))

        @blk.tensor
        def _(t):
            dv.wait(t, ln_done)
            for m in range(3):
                wls[0].wait_k(t, NCH * (m + 1))
                wb_ = wt0
                for oc in range(NCH):
                    for tc in range(TCH):
                        j = (m * NCH + oc) * TCH + tc
                        if j >= 4:
                            dv.wait(t, dvM0 + (j - 4) + 1)
                        psx = psl[j % 4]
                        rg = 0
                        inst = None
                        if m < 2:
                            for c in range(NCH):
                                inst = t.matmul(
                                    psx[:, rg:rg + 128],
                                    wb_[:, c * D + oc * 128: c * D + oc * 128 + 128],
                                    xa[:, c * T + tc * 128: c * T + (tc + 1) * 128],
                                    start=(c == 0), stop=(c == NCH - 1))
                        else:
                            for c in range(NCH):
                                inst = t.matmul(
                                    psx[:, rg:rg + 128],
                                    xa[:, c * T + tc * 128: c * T + (tc + 1) * 128],
                                    wb_[:, c * D + oc * 128: c * D + oc * 128 + 128],
                                    start=(c == 0), stop=(c == NCH - 1))
                        pe.inc(inst)
                        _n16(t)

        @blk.vector
        def _(v):
            v = b.wrap(v, "dve")
            for m in range(3):
                for oc in range(NCH):
                    row = oc % 2
                    if m * NCH + oc >= 2:
                        sto[row].wait_k(v._e, ((m * NCH + oc) - 2) // 2 + 1)
                    for tc in range(TCH):
                        j = (m * NCH + oc) * TCH + tc
                        pe.wait(v._e, peM0 + j + 1)
                        psx = psl[j % 4]
                        rg = 0
                        v.tensor_copy(ost[:, row * T + tc * 128: row * T + (tc + 1) * 128],
                                      psx[:, rg:rg + 128])
                        b.mk(v, dv)

        @blk.scalar
        def _(sc):
            for m in range(3):
                for oc in range(NCH):
                    row = oc % 2
                    dv.wait(sc, dvM0 + (m * NCH + oc + 1) * TCH)
                    if m < 2:
                        dst = outs[m][oc, :, :, :]
                    else:
                        dst = outs[2][:, oc, :, :]
                    sto[row].issue(sc.dma_start(
                        out=dst.rearrange("tc p t -> p tc t"),
                        in_=ost[:, row * T: row * T + T].rearrange("p (tc t) -> p tc t", tc=TCH)))

        @blk.sync
        def _(s):
            sto[0].wait_k(s, sto[0].n)
            sto[1].wait_k(s, sto[1].n)
        es_xa.close()
        b.check()
    es.close()
    return nc



def _ext(qt):
    return TCH + qt + 1


def build_attn0_mix(stop_flash=None, proj_tc=None):
    KB = 2 * TCH
    nc = bass.Bass("TRN2", target_bir_lowering=False, debug=False)
    x_in = nc.dram_tensor("x_in", [D, T], F32, kind="ExternalInput").ap()
    x_out = nc.dram_tensor("x_out", [D, T], F32, kind="ExternalOutput").ap()
    q_t = nc.dram_tensor("q_t", [NCH, TCH, 128, 128], BF16, kind="ExternalInput").ap()
    k_t = nc.dram_tensor("k_t", [NCH, KB, 128, 128], BF16, kind="ExternalInput").ap()
    v_t = nc.dram_tensor("v_t", [KB, NCH, 128, 128], BF16, kind="ExternalInput").ap()
    mask_i = nc.dram_tensor("mask", [128, MASKW], F32, kind="ExternalInput").ap()
    ident_i = nc.dram_tensor("ident", [128, 128], BF16, kind="ExternalInput").ap()
    identf_i = nc.dram_tensor("identf", [128, 128], F32, kind="ExternalInput").ap()
    wo = nc.dram_tensor("wo", [D, D], F32, kind="ExternalInput").ap()
    wo_b = nc.dram_tensor("wo_b", [D, D], BF16, kind="Internal").ap()
    z_d = nc.dram_tensor("z_d", [NCH, TCH, 128, 128], BF16, kind="Internal").ap()

    es = ExitStack()
    b = B(nc, es)
    ident = b.sb("identt", [128, 128], BF16)
    identf = b.sb("identft", [128, 128], F32)
    onesrf = b.sb("onesrf", [1, 128], F32)
    es_fl = ExitStack()
    b.es = es_fl
    mask = b.sb("maskt", [128, MASKW], F32)
    kbuf = b.sb("kbuf", [128, 2 * KB * 128], BF16)
    vbuf = b.sb("vbuf", [128, 2 * KB * 128], BF16)
    qtl = b.sb("qtl", [128, 2 * 128], BF16)
    sS = b.sb("sS", [128, KB * 128], F32)
    pbuf = b.sb("pbuf", [128, KB * 128], BF16)
    ptb = b.sb("ptb", [128, KB * 128], BF16)
    lsp = b.sb("lsp", [128, 8], F32)
    znrm = b.sb("znrm", [128, 280], F32)
    zst = b.sb("zst", [128, 256], BF16)
    b.es = es

    ps1 = b.ps("ps1", [128, 512], F32)
    ps2 = b.ps("ps2", [128, 512], F32)
    ps3 = b.ps("ps3", [128, 512], F32)
    ps4 = b.ps("ps4", [128, 512], F32)
    ps5 = b.ps("ps5", [128, 512], F32)
    ps6 = b.ps("ps6", [128, 512], F32)

    M = Marks()
    NIT = NCH * TCH
    NS = NIT if stop_flash is None else stop_flash
    PT = TCH if proj_tc is None else proj_tc

    def rounds(e):
        return (e + 7) // 8

    def sec_loads(s, pe, dv, ac, csem, ldc, kvl, qls):
        ldc.issue(s.dma_start(out=ident[:, :], in_=ident_i[:, :]))
        ldc.issue(s.dma_start(out=identf[:, :], in_=identf_i[:, :]))
        ldc.issue(s.dma_start(out=mask[:, :], in_=mask_i[:, :]))
        for hh in range(NCH):
            if hh * TCH >= NS:
                continue
            if hh >= 2:
                pe.wait(s, M.get(("kvused", hh - 2)))
            kvl[hh % 2].issue(s.dma_start(
                out=kbuf[:, (hh % 2) * KB * 128:(hh % 2 + 1) * KB * 128]
                .rearrange("p (kb t) -> p kb t", kb=KB),
                in_=k_t[hh, :, :, :].rearrange("kb p t -> p kb t")))
            kvl[hh % 2].issue(s.dma_start(
                out=vbuf[:, (hh % 2) * KB * 128:(hh % 2 + 1) * KB * 128]
                .rearrange("p (kb f) -> p kb f", kb=KB),
                in_=v_t[:, hh, :, :].rearrange("kb p f -> p kb f")))
            for qt in range(TCH):
                i = hh * TCH + qt
                if i >= NS:
                    continue
                if i >= 2:
                    pe.wait(s, M.get((i - 2, "qused")))
                qls[i % 2].issue(s.dma_start(
                    out=qtl[:, (i % 2) * 128:(i % 2) * 128 + 128],
                    in_=q_t[hh, qt, :, :]))

    def sec_pe(t, pe, dv, ac, csem, ldc, kvl, qls, zss):
        ldc.wait_k(t, 3)
        for hh in range(NCH):
            if hh * TCH >= NS:
                continue
            kvl[hh % 2].wait_k(t, 2 * (hh // 2 + 1))
            kb_base = (hh % 2) * KB * 128
            for qt in range(TCH):
                i = hh * TCH + qt
                if i >= NS:
                    continue
                e = _ext(qt)
                qls[i % 2].wait_k(t, i // 2 + 1)
                qv = qtl[:, (i % 2) * 128:(i % 2) * 128 + 128]
                for kb in range(e):
                    if kb >= 8:
                        dv.wait(t, M.get((i, "sevr", kb % 2, kb // 8 - 1)))
                    elif i >= 1 and kb < 2:
                        dv.wait(t, M.get((i - 1, "sevlast", kb % 2)))
                    psx = [ps1, ps2][kb % 2]
                    inst = t.matmul(psx[:, ((kb // 2) % 4) * 128:((kb // 2) % 4) * 128 + 128],
                                    qv, kbuf[:, kb_base + kb * 128: kb_base + (kb + 1) * 128],
                                    start=True, stop=True)
                    pe.inc(inst)
                    _n16(t)
                    M.set((i, "smm", kb), pe.n)
                M.set((i, "qused"), pe.n)
                for kb in range(e):
                    ac.wait(t, M.get((i, "exp", kb // 4)))
                    if kb >= 8:
                        ac.wait(t, M.get((i, "ptevr", kb % 2, kb // 8 - 1)))
                    elif i >= 1 and kb < 2:
                        ac.wait(t, M.get((i - 1, "ptevlast", kb % 2)))
                    psx = [ps3, ps4][kb % 2]
                    inst = t.matmul(psx[:, ((kb // 2) % 4) * 128:((kb // 2) % 4) * 128 + 128],
                                    pbuf[:, kb * 128:(kb + 1) * 128], ident[:, :],
                                    start=True, stop=True)
                    pe.inc(inst)
                    _n16(t)
                    M.set((i, "ptmm", kb), pe.n)
                ac.wait(t, M.get((i, "ptevall")))
                if i >= 1:
                    dv.wait(t, M.get((i - 1, "zev")))
                inst = None
                for kb in range(e):
                    inst = t.matmul(ps5[:, 0:128],
                                    vbuf[:, kb_base + kb * 128: kb_base + (kb + 1) * 128],
                                    ptb[:, kb * 128:(kb + 1) * 128],
                                    start=(kb == 0), stop=(kb == e - 1))
                pe.inc(inst)
                _n16(t)
                M.set((i, "pv"), pe.n)
                ac.wait(t, M.get((i, "lrec")))
                if i >= 1:
                    ac.wait(t, M.get((i - 1, "lbc")))
                inst = t.matmul(ps6[0:1, 0:128], znrm[:, 1:2], identf[:, :],
                                start=True, stop=True)
                pe.inc(inst)
                _n16(t)
                M.set((i, "lrow"), pe.n)
                ac.wait(t, M.get((i, "lrowev")))
                inst = t.matmul(ps6[:, 128:256], onesrf[0:1, :], znrm[0:1, 136:264],
                                start=True, stop=True)
                pe.inc(inst)
                _n16(t)
                M.set((i, "lbcmm"), pe.n)
            M.set(("kvused", hh), pe.n)
        M.set(("flashpe",), pe.n)

    def sec_dve(v, pe, dv, ac, csem, zss):
        for hh in range(NCH):
            for qt in range(TCH):
                i = hh * TCH + qt
                if i >= NS:
                    continue
                e = _ext(qt)
                moff = MASK_OFF - 128 * qt
                for r in range(rounds(e)):
                    for q in range(2):
                        kbs = [kb for kb in range(8 * r + q, min(8 * r + 8, e), 2)]
                        if not kbs:
                            M.set((i, "sevr", q, r), dv.n)
                            continue
                        pe.wait(v._e, M.get((i, "smm", kbs[-1])))
                        psx = [ps1, ps2][q]
                        for idx, kb in enumerate(kbs):
                            v.tensor_tensor(
                                sS[:, kb * 128:(kb + 1) * 128],
                                psx[:, ((kb // 2) % 4) * 128:((kb // 2) % 4) * 128 + 128],
                                mask[:, moff + kb * 128: moff + (kb + 1) * 128],
                                ALU.add)
                        b.mk(v, dv)
                        M.set((i, "sevr", q, r), dv.n)
                for q in range(2):
                    M.set((i, "sevlast", q), M.get((i, "sevr", q, rounds(e) - 1)))
                v.reduce_max(znrm[:, 4:5], sS[:, 0:e * 128:8], axis=AX.X, negate=True)
                v.tensor_scalar(znrm[:, 5:6], znrm[:, 4:5], SM_SCALE, None, ALU.mult)
                b.mk(v, dv)
                M.set((i, "nm2"), dv.n)
                nsp = (e + 3) // 4
                ac.wait(v._e, M.get((i, "expall")))
                v.reduce_sum(znrm[:, 0:1], lsp[:, 0:nsp], axis=AX.X)
                b.mk(v, dv)
                M.set((i, "lacc"), dv.n)
                pe.wait(v._e, M.get((i, "pv")))
                ac.wait(v._e, M.get((i, "lbc")))
                if i >= 2:
                    zss[i % 2].wait_k(v._e, i // 2)
                v.tensor_tensor(zst[:, (i % 2) * 128:(i % 2) * 128 + 128],
                                ps5[:, 0:128], znrm[:, 8:136], ALU.mult)
                b.mk(v, dv)
                M.set((i, "zev"), dv.n)
        M.set(("flashdv",), dv.n)

    def sec_act(sc, pe, dv, ac, csem, zss):
        for hh in range(NCH):
            for qt in range(TCH):
                i = hh * TCH + qt
                if i >= NS:
                    continue
                e = _ext(qt)
                nsp = (e + 3) // 4
                dv.wait(sc._e, M.get((i, "nm2")))
                for sp in range(nsp):
                    k0 = 4 * sp
                    k1 = min(4 * sp + 4, e)
                    sc.activation(pbuf[:, k0 * 128:k1 * 128], sS[:, k0 * 128:k1 * 128],
                                  AF.Exp, scale=SM_SCALE, bias=znrm[:, 5:6],
                                  accum_out=lsp[:, sp:sp + 1])
                    b.mk(sc, ac)
                    M.set((i, "exp", sp), ac.n)
                M.set((i, "expall"), ac.n)
                for r in range(rounds(e)):
                    for q in range(2):
                        kbs = [kb for kb in range(8 * r + q, min(8 * r + 8, e), 2)]
                        if not kbs:
                            M.set((i, "ptevr", q, r), ac.n)
                            continue
                        pe.wait(sc._e, M.get((i, "ptmm", kbs[-1])))
                        psx = [ps3, ps4][q]
                        n = len(kbs)
                        pv_ = psx[:, 0:n * 128]
                        w = 2 * (n - 1) + 1
                        if n > 1:
                            outv = ptb[:, kbs[0] * 128:(kbs[0] + w) * 128].rearrange(
                                "p (kb t) -> p kb t", t=128)[:, 0:w:2, :]
                        else:
                            outv = ptb[:, kbs[0] * 128:kbs[0] * 128 + 128]
                        sc.copy(outv, pv_)
                        b.mk(sc, ac)
                        M.set((i, "ptevr", q, r), ac.n)
                for q in range(2):
                    M.set((i, "ptevlast", q), M.get((i, "ptevr", q, rounds(e) - 1)))
                M.set((i, "ptevall"), ac.n)
                dv.wait(sc._e, M.get((i, "lacc")))
                sc.activation(znrm[:, 1:2], znrm[:, 0:1], AF.Ln)
                sc.activation(znrm[:, 1:2], znrm[:, 1:2], AF.Exp, scale=-1.0)
                b.mk(sc, ac)
                M.set((i, "lrec"), ac.n)
                pe.wait(sc._e, M.get((i, "lrow")))
                sc.copy(znrm[0:1, 136:264], ps6[0:1, 0:128])
                b.mk(sc, ac)
                M.set((i, "lrowev"), ac.n)
                pe.wait(sc._e, M.get((i, "lbcmm")))
                sc.copy(znrm[:, 8:136], ps6[:, 128:256])
                b.mk(sc, ac)
                M.set((i, "lbc"), ac.n)
                dv.wait(sc._e, M.get((i, "zev")))
                zss[i % 2].issue(sc.dma_start(
                    out=z_d[hh, qt, :, :],
                    in_=zst[:, (i % 2) * 128:(i % 2) * 128 + 128]))

    for _ in range(1):
        pe = Counter(None); dv = Counter(None); ac = Counter(None)
        csem = DSem(None); ldc = DSem(None)
        kvl = [DSem(None), DSem(None)]
        qls = [DSem(None), DSem(None)]
        zss = [DSem(None), DSem(None)]
        mock = MockEng()
        object.__setattr__(mock, "_e", mock) if False else None
        mock.__dict__["_e"] = mock
        mock.__dict__["_c"] = Counter(None)
        sec_pe(mock, pe, dv, ac, csem, ldc, kvl, qls, zss)
        sec_dve(mock, pe, dv, ac, csem, zss)
        sec_act(mock, pe, dv, ac, csem, zss)
        sec_loads(mock, pe, dv, ac, csem, ldc, kvl, qls)
    M.plan = False

    with nc.Block() as blk:
        b.block = blk
        pe = b.sem("pe")
        dv = b.sem("dv")
        ac = b.sem("ac")
        csem = b.dsem("csem")
        ldc = b.dsem("ldc")
        kvl = [b.dsem("kvl0"), b.dsem("kvl1")]
        qls = [b.dsem("qls0"), b.dsem("qls1")]
        zss = [b.dsem("zss0"), b.dsem("zss1")]

        plc = b.sem("plc")

        @blk.gpsimd
        def _(g):
            gw = b.wrap(g, "pool")
            csem.issue(g.dma_start(out=wo_b[:, :], in_=wo[:, :]))
            gw.memset(onesrf[:, :], 1.0)
            b.mk(gw, plc)
        ones_pl = plc.n

        @blk.sync
        def _(s):
            sec_loads(s, pe, dv, ac, csem, ldc, kvl, qls)

        @blk.tensor
        def _(t):
            plc.wait(t, ones_pl)
            sec_pe(t, pe, dv, ac, csem, ldc, kvl, qls, zss)

        @blk.vector
        def _(v):
            v = b.wrap(v, "dve")
            sec_dve(v, pe, dv, ac, csem, zss)

        @blk.scalar
        def _(sc):
            sc = b.wrap(sc, "act")
            sec_act(sc, pe, dv, ac, csem, zss)

        if stop_flash is not None:
            dbg = b.dsem("dbg")

            @blk.gpsimd
            def _(g):
                pe.wait(g, M.get(("flashpe",)))
                dv.wait(g, M.get(("flashdv",)))
                dbg.issue(g.dma_start(out=x_out[0:128, 0:256], in_=zst[:, :]))

            @blk.sync
            def _(s):
                zss[0].wait_k(s, zss[0].n)
                zss[1].wait_k(s, zss[1].n)
                dbg.wait_k(s, 1)
            b.check()
            es_fl.close()
            es.close()
            return nc

        es_fl.close()
        es_p = ExitStack()
        b.es = es_p
        wproj = b.sb("wproj", [128, NCH * D], BF16)
        zcol = b.sb("zcol", [128, 2 * NCH * 128], BF16)
        xcol = b.sb("xcol", [128, 2 * NCH * 128], F32)
        oute = b.sb("oute", [128, 2 * NCH * 128], F32)
        b.es = es
        wpl = b.dsem("wpl")
        zcl = [b.dsem("zcl0"), b.dsem("zcl1")]
        xrl = [b.dsem("xrl0"), b.dsem("xrl1")]
        sto = [b.dsem("sto0"), b.dsem("sto1")]
        flashpe = M.get(("flashpe",))
        flashdv = M.get(("flashdv",))
        peP0 = pe.n
        dvP0 = dv.n

        @blk.sync
        def _(s):
            pe.wait(s, flashpe)
            dv.wait(s, flashdv)
            zss[0].wait_k(s, zss[0].n)
            zss[1].wait_k(s, zss[1].n)
            csem.wait_k(s, 1)
            for c in range(NCH):
                wpl.issue(s.dma_start_transpose(
                    wproj[:, c * D:(c + 1) * D], wo_b[0:D, c * 128:(c + 1) * 128]))
            for tc in range(PT):
                if tc >= 2:
                    pe.wait(s, peP0 + (tc - 1) * NCH)
                    dv.wait(s, dvP0 + (tc - 1) * NCH)
                zcl[tc % 2].issue(s.dma_start(
                    out=zcol[:, (tc % 2) * NCH * 128:(tc % 2 + 1) * NCH * 128]
                    .rearrange("p (c t) -> p c t", c=NCH),
                    in_=z_d[:, tc, :, :].rearrange("c p t -> p c t")))
                xrl[tc % 2].issue(s.dma_start(
                    out=xcol[:, (tc % 2) * NCH * 128:(tc % 2 + 1) * NCH * 128]
                    .rearrange("p (c t) -> p c t", c=NCH),
                    in_=x_in.rearrange("(c p) t -> p c t", p=128)[:, :, tc * 128:(tc + 1) * 128]))

        @blk.tensor
        def _(t):
            wpl.wait_k(t, NCH)
            for tc in range(PT):
                zcl[tc % 2].wait_k(t, tc // 2 + 1)
                zb = zcol[:, (tc % 2) * NCH * 128:(tc % 2 + 1) * NCH * 128]
                for oc in range(NCH):
                    j = tc * NCH + oc
                    if j >= 4:
                        dv.wait(t, dvP0 + (j - 4) + 1)
                    psx = [ps1, ps2, ps3, ps4][j % 4]
                    rg = 0
                    inst = None
                    for c in range(NCH):
                        inst = t.matmul(psx[:, rg:rg + 128],
                                        wproj[:, c * D + oc * 128: c * D + oc * 128 + 128],
                                        zb[:, c * 128:(c + 1) * 128],
                                        start=(c == 0), stop=(c == NCH - 1))
                    pe.inc(inst)
                    _n16(t)

        @blk.vector
        def _(v):
            v = b.wrap(v, "dve")
            for tc in range(PT):
                xrl[tc % 2].wait_k(v._e, tc // 2 + 1)
                if tc >= 2:
                    sto[tc % 2].wait_k(v._e, tc // 2)
                for oc in range(NCH):
                    j = tc * NCH + oc
                    pe.wait(v._e, peP0 + j + 1)
                    psx = [ps1, ps2, ps3, ps4][j % 4]
                    rg = 0
                    v.tensor_tensor(
                        oute[:, (tc % 2) * NCH * 128 + oc * 128:
                             (tc % 2) * NCH * 128 + oc * 128 + 128],
                        psx[:, rg:rg + 128],
                        xcol[:, (tc % 2) * NCH * 128 + oc * 128:
                             (tc % 2) * NCH * 128 + oc * 128 + 128], ALU.add)
                    b.mk(v, dv)

        @blk.scalar
        def _(sc):
            for tc in range(PT):
                dv.wait(sc, dvP0 + (tc + 1) * NCH)
                sto[tc % 2].issue(sc.dma_start(
                    out=x_out.rearrange("(c p) t -> p c t", p=128)[:, :, tc * 128:(tc + 1) * 128],
                    in_=oute[:, (tc % 2) * NCH * 128:(tc % 2 + 1) * NCH * 128]
                    .rearrange("p (c t) -> p c t", c=NCH)))

        @blk.sync
        def _(s):
            sto[0].wait_k(s, sto[0].n)
            sto[1].wait_k(s, sto[1].n)
        es_p.close()
        b.check()
    es.close()
    return nc




ROTC = 4


def build_attn1_front():
    nc = bass.Bass("TRN2", target_bir_lowering=False, debug=False)
    x_in = nc.dram_tensor("x_in", [D, T], F32, kind="ExternalInput").ap()
    w_q = nc.dram_tensor("w_q", [D, D], F32, kind="ExternalInput").ap()
    w_k = nc.dram_tensor("w_k", [D, D], F32, kind="ExternalInput").ap()
    w_v = nc.dram_tensor("w_v", [D, D], F32, kind="ExternalInput").ap()
    gam_i = nc.dram_tensor("gam", [128, NCH], F32, kind="ExternalInput").ap()
    bet_i = nc.dram_tensor("bet", [128, NCH], F32, kind="ExternalInput").ap()
    bq_i = nc.dram_tensor("bq", [128, NCH], F32, kind="ExternalInput").ap()
    bk_i = nc.dram_tensor("bk", [128, NCH], F32, kind="ExternalInput").ap()
    bv_i = nc.dram_tensor("bv", [128, NCH], F32, kind="ExternalInput").ap()
    cost_i = nc.dram_tensor("cost", [128, ROTC * T], F32, kind="ExternalInput").ap()
    sint_i = nc.dram_tensor("sint", [128, ROTC * T], F32, kind="ExternalInput").ap()
    q5_d = nc.dram_tensor("q5", [NCH, TCH, 128, 128], F32, kind="ExternalOutput").ap()
    ar_d = nc.dram_tensor("ar", [1, T], F32, kind="ExternalOutput").ap()
    br_d = nc.dram_tensor("br", [1, T], F32, kind="ExternalOutput").ap()
    a_part = nc.dram_tensor("a_part", [128, NCH], F32, kind="ExternalOutput").ap()
    wqb = nc.dram_tensor("wqb", [D, D], BF16, kind="Internal").ap()
    wkb = nc.dram_tensor("wkb", [D, D], BF16, kind="Internal").ap()
    wvb = nc.dram_tensor("wvb", [D, D], BF16, kind="Internal").ap()
    kv_d = nc.dram_tensor("kv_d", [2, NCH, TCH, 128, 128], F32, kind="Internal").ap()

    es = ExitStack()
    b = B(nc, es)
    xn2b = b.sb("xn2b", [128, NCH * T], BF16)
    onesc = b.sb("onesc", [128, 1], F32)
    onesrf = b.sb("onesrf", [1, 128], F32)
    A_acc = b.sb("A_acc", [128, NCH], F32)
    nrow = b.sb("nrow", [1, 3 * T], F32)
    gamc = b.sb("gamc", [128, NCH], F32)
    betc = b.sb("betc", [128, NCH], F32)
    bqc = b.sb("bqc", [128, NCH], F32)
    bkc = b.sb("bkc", [128, NCH], F32)
    bvc = b.sb("bvc", [128, NCH], F32)
    ps1 = b.ps("ps1", [128, 512], F32)
    ps2 = b.ps("ps2", [128, 512], F32)
    ps3 = b.ps("ps3", [128, 512], F32)
    ps4 = b.ps("ps4", [128, 512], F32)
    ps5 = b.ps("ps5", [128, 512], F32)
    ps6 = b.ps("ps6", [128, 512], F32)
    psc = [ps1, ps2, ps3, ps4]
    pss = [ps5, ps6]

    es_t = ExitStack()
    b.es = es_t
    a_all = b.sb("a_all", [128, T], F32)
    b_all = b.sb("b_all", [128, T], F32)
    xn4b = b.sb("xn4b", [128, NCH * T], BF16)
    u_t = b.sb("u_t", [128, 2 * 128], F32)
    s2_t = b.sb("s2_t", [128, 2 * 128], F32)
    tr_t = b.sb("tr_t", [128, 128], F32)
    b.es = es

    M = Marks()
    ordkv = [0, 4, 1, 5, 2, 6, 3, 7, 8, 9, 10, 11, 12, 13, 14, 15]
    ords = [list(range(NCH)), ordkv, ordkv]
    wmats = [wqb, wkb, wvb]

    def sec_sync(s, c):
        (pe, dv, ac, plc, csem, ldc, rst, wls, st4, pld, ast, csl) = c
        ldc.issue(s.dma_start(out=gamc[:, :], in_=gam_i[:, :]))
        ldc.issue(s.dma_start(out=betc[:, :], in_=bet_i[:, :]))
        ldc.issue(s.dma_start(out=bqc[:, :], in_=bq_i[:, :]))
        ldc.issue(s.dma_start(out=bkc[:, :], in_=bk_i[:, :]))
        ldc.issue(s.dma_start(out=bvc[:, :], in_=bv_i[:, :]))
        dv.wait(s, M.get(("lnd",)))
        rst.issue(s.dma_start(out=ar_d[:, :], in_=a_all[0:1, :]))
        rst.issue(s.dma_start(out=br_d[:, :], in_=b_all[0:1, :]))
        dv.wait(s, M.get(("xn2all",)))
        rst.wait_k(s, 2)
        csem.wait_k(s, 3)

        def emit_store(j):
            m_, pos_ = divmod(j, NCH)
            oc_ = ords[m_][pos_]
            if m_ == 0:
                ac.wait(s, M.get(("evcol", j)))
                dst = q5_d[oc_, :, :, :]
            else:
                if pos_ < 8:
                    dv.wait(s, M.get(("rotcol", j)))
                else:
                    ac.wait(s, M.get(("evcol", j)))
                dst = kv_d[m_ - 1, oc_, :, :, :]
            st4[j % 4].issue(s.dma_start(
                out=dst.rearrange("tc p t -> p tc t"),
                in_=stag[:, (j % 4) * T:(j % 4) * T + T]
                .rearrange("p (tc t) -> p tc t", tc=TCH)))

        for m in range(3):
            for pos in range(NCH):
                idx = m * NCH + pos
                oc = ords[m][pos]
                if idx >= 2:
                    emit_store(idx - 2)
                if m >= 1 and pos < 8 and pos % 2 == 0:
                    l = (m - 1) * ROTC + pos // 2
                    if l >= 2:
                        dv.wait(s, M.get(("rotpair", l - 2)))
                    csl[l % 2].issue(s.dma_start(
                        out=cosb[:, (l % 2) * T:(l % 2 + 1) * T],
                        in_=cost_i[:, (pos // 2) * T:(pos // 2 + 1) * T]))
                    csl[l % 2].issue(s.dma_start(
                        out=sinb[:, (l % 2) * T:(l % 2 + 1) * T],
                        in_=sint_i[:, (pos // 2) * T:(pos // 2 + 1) * T]))
                if idx >= 2:
                    pe.wait(s, M.get(("mmcol", idx - 2)))
                wls[idx % 2].issue(s.dma_start(
                    out=wcol[:, (idx % 2) * NCH * 128:(idx % 2 + 1) * NCH * 128]
                    .rearrange("p (c o) -> p c o", c=NCH),
                    in_=wmats[m].rearrange("(c p) o -> p c o", p=128)[:, :, oc * 128:(oc + 1) * 128]))
        emit_store(3 * NCH - 2)
        emit_store(3 * NCH - 1)
        for q_ in range(4):
            st4[q_].wait_k(s, st4[q_].n)
        for tc in range(TCH):
            if tc >= 2:
                dv.wait(s, M.get(("p2", tc - 2)))
            pld[tc % 2].issue(s.dma_start(
                out=kcol[:, (tc % 2) * NCH * 128:(tc % 2 + 1) * NCH * 128]
                .rearrange("p (c t) -> p c t", c=NCH),
                in_=kv_d[0, :, tc, :, :].rearrange("c p t -> p c t")))
            pld[tc % 2].issue(s.dma_start(
                out=vcol[:, (tc % 2) * NCH * 128:(tc % 2 + 1) * NCH * 128]
                .rearrange("p (c t) -> p c t", c=NCH),
                in_=kv_d[1, :, tc, :, :].rearrange("c p t -> p c t")))
        dv.wait(s, M.get(("p2", TCH - 1)))
        ast.issue(s.dma_start(out=a_part[:, :], in_=A_acc[:, :]))
        ast.wait_k(s, 1)
        rst.wait_k(s, 2)

    def sec_pe(t, c):
        (pe, dv, ac, plc, csem, ldc, rst, wls, st4, pld, ast, csl) = c
        plc.wait(t, M.get(("plc",)))
        ct = 0
        for m in range(3):
            for pos in range(NCH):
                idx = m * NCH + pos
                oc = ords[m][pos]
                wls[idx % 2].wait_k(t, idx // 2 + 1)
                if idx == 0:
                    dv.wait(t, M.get(("xn2all",)))
                wb = wcol[:, (idx % 2) * NCH * 128:(idx % 2 + 1) * NCH * 128]
                for tc in range(TCH):
                    jj = idx * TCH + tc
                    if jj >= 4:
                        ac.wait(t, M.get(("ev", jj - 4)))
                    inst = None
                    for cc in range(NCH):
                        inst = t.matmul(psc[jj % 4][:, 0:128],
                                        wb[:, cc * 128:(cc + 1) * 128],
                                        xn2b[:, cc * T + tc * 128: cc * T + (tc + 1) * 128],
                                        start=(cc == 0), stop=(cc == NCH - 1))
                    pe.inc(inst)
                    _n16(t)
                    M.set(("mm", jj), pe.n)
                M.set(("mmcol", idx), pe.n)
                sq_cols = []
                if m >= 1:
                    if pos < 8:
                        if pos % 2 == 1:
                            sq_cols = [pos - 1, pos]
                    else:
                        sq_cols = [pos]
                for pc in sq_cols:
                    pidx = m * NCH + pc
                    for tc in range(TCH):
                        ac.wait(t, M.get(("sq", pidx, tc)))
                        if ct >= 2:
                            dv.wait(t, M.get(("sreldv", ct - 2)))
                            ac.wait(t, M.get(("srelac", ct - 2)))
                        inst = t.matmul(pss[ct % 2][0:1, 0:128], onesc[:, :],
                                        sq2[:, (tc % 2) * 128:(tc % 2) * 128 + 128],
                                        start=True, stop=True)
                        pe.inc(inst)
                        _n16(t)
                        M.set(("ones", pidx, tc), pe.n)
                        M.set(("onesct", pidx, tc), ct)
                        ct += 1
        for tc in range(TCH):
            ac.wait(t, M.get(("wrow",)))
            if ct >= 2:
                dv.wait(t, M.get(("sreldv", ct - 2)))
                ac.wait(t, M.get(("srelac", ct - 2)))
            inst = t.matmul(pss[ct % 2][:, 0:128], onesrf[0:1, :],
                            nrow[0:1, 2 * T + tc * 128: 2 * T + (tc + 1) * 128],
                            start=True, stop=True)
            pe.inc(inst)
            _n16(t)
            M.set(("wbc", tc), pe.n)
            M.set(("wbcct", tc), ct)
            ct += 1

    def sec_act(sc, c):
        (pe, dv, ac, plc, csem, ldc, rst, wls, st4, pld, ast, csl) = c
        for tc in range(TCH):
            dv.wait(sc._e, M.get(("s2pre", tc)))
            sc.activation(s2_t[:, (tc % 2) * 128:(tc % 2) * 128 + 128],
                          u_t[:, (tc % 2) * 128:(tc % 2) * 128 + 128], AF.Ln)
            sc.activation(s2_t[:, (tc % 2) * 128:(tc % 2) * 128 + 128],
                          s2_t[:, (tc % 2) * 128:(tc % 2) * 128 + 128],
                          AF.Exp, scale=-0.5)
            b.mk(sc, ac)
            M.set(("s2", tc), ac.n)
        sqct = 0
        for m in range(3):
            bcol = [bqc, bkc, bvc][m]
            for pos in range(NCH):
                idx = m * NCH + pos
                oc = ords[m][pos]
                for tc in range(TCH):
                    jj = idx * TCH + tc
                    pe.wait(sc._e, M.get(("mm", jj)))
                    if idx >= 4:
                        pidx = idx - 4
                        st4[idx % 4].wait_k(sc._e, idx // 4)
                        if pidx % NCH < 8 and pidx >= NCH:
                            dv.wait(sc._e, M.get(("rotcol", pidx)))
                    sc.activation(stag[:, (idx % 4) * T + tc * 128:
                                       (idx % 4) * T + (tc + 1) * 128],
                                  psc[jj % 4][:, 0:128], AF.Identity,
                                  bias=bcol[:, oc:oc + 1])
                    b.mk(sc, ac)
                    M.set(("ev", jj), ac.n)
                M.set(("evcol", idx), ac.n)
                sq_cols = []
                if m >= 1:
                    if pos < 8:
                        if pos % 2 == 1:
                            sq_cols = [pos - 1, pos]
                    else:
                        sq_cols = [pos]
                for pc in sq_cols:
                    pidx = m * NCH + pc
                    for tc in range(TCH):
                        if m >= 1 and pc < 8:
                            dv.wait(sc._e, M.get(("rot", m * NCH + (pos | 1), tc)))
                        if sqct >= 2:
                            pe.wait(sc._e, M.get(("ones_at", sqct - 2)))
                        sc.activation(sq2[:, (tc % 2) * 128:(tc % 2) * 128 + 128],
                                      stag[:, (pidx % 4) * T + tc * 128:
                                           (pidx % 4) * T + (tc + 1) * 128],
                                      AF.Square)
                        b.mk(sc, ac)
                        M.set(("sq", pidx, tc), ac.n)
                        M.set(("sq_at", sqct), (pidx, tc))
                        sqct += 1
        dv.wait(sc._e, M.get(("nw",)))
        sc.activation(nrow[0:1, 2 * T:3 * T], nrow[0:1, 2 * T:3 * T], AF.Ln)
        sc.activation(nrow[0:1, 2 * T:3 * T], nrow[0:1, 2 * T:3 * T],
                      AF.Exp, scale=-0.5)
        b.mk(sc, ac)
        M.set(("wrow",), ac.n)
        for tc in range(TCH):
            pe.wait(sc._e, M.get(("wbc", tc)))
            if tc >= 2:
                dv.wait(sc._e, M.get(("p2", tc - 2)))
            sc.copy(wbct[:, (tc % 2) * 128:(tc % 2) * 128 + 128],
                    pss[M.get(("wbcct", tc)) % 2][:, 0:128])
            b.mk(sc, ac)
            M.set(("wbce", tc), ac.n)

    def sec_dve(v, c):
        (pe, dv, ac, plc, csem, ldc, rst, wls, st4, pld, ast, csl) = c
        for tc in range(TCH):
            if tc == 0:
                dv.wait(v._e, M.get(("lnd",)))
                ldc.wait_k(v._e, 5)
            if tc >= 2:
                ac.wait(v._e, M.get(("s2", tc - 2)))
            ut = u_t[:, (tc % 2) * 128:(tc % 2) * 128 + 128]
            at = a_all[:, tc * 128:(tc + 1) * 128]
            v.tensor_tensor(ut, at, at, ALU.mult)
            v.tensor_scalar(ut, ut, -EPS, 1.0 + EPS, ALU.mult, ALU.add)
            b.mk(v, dv)
            M.set(("s2pre", tc), dv.n)
            ac.wait(v._e, M.get(("s2", tc)))
            st = s2_t[:, (tc % 2) * 128:(tc % 2) * 128 + 128]
            for cc in range(NCH):
                v.tensor_tensor(tr_t[:, :],
                                xn4b[:, cc * T + tc * 128: cc * T + (tc + 1) * 128],
                                st, ALU.mult)
                v.tensor_scalar(tr_t[:, :], tr_t[:, :], gamc[:, cc:cc + 1], None, ALU.mult)
                v.tensor_scalar(xn2b[:, cc * T + tc * 128: cc * T + (tc + 1) * 128],
                                tr_t[:, :], betc[:, cc:cc + 1], None, ALU.add)
            b.mk(v, dv)
            M.set(("xn2", tc), dv.n)
        M.set(("xn2all",), dv.n)
        nct = 0
        for m in range(1, 3):
            for pos in range(NCH):
                idx = m * NCH + pos
                if pos < 8 and pos % 2 == 1:
                    fch = ords[m][pos - 1]
                    l = (m - 1) * ROTC + fch
                    sE = (idx - 1) % 4
                    sO = idx % 4
                    for tc in range(TCH):
                        if tc == 0:
                            csl[l % 2].wait_k(v._e, 2 * (l // 2 + 1))
                        ac.wait(v._e, M.get(("ev", (idx - 1) * TCH + tc)))
                        ac.wait(v._e, M.get(("ev", idx * TCH + tc)))
                        ke = stag[:, sE * T + tc * 128: sE * T + (tc + 1) * 128]
                        ko = stag[:, sO * T + tc * 128: sO * T + (tc + 1) * 128]
                        cc_ = cosb[:, (l % 2) * T + tc * 128: (l % 2) * T + (tc + 1) * 128]
                        ss_ = sinb[:, (l % 2) * T + tc * 128: (l % 2) * T + (tc + 1) * 128]
                        v.tensor_tensor(rt1[:, :], ke, cc_, ALU.mult)
                        v.tensor_tensor(rt2[:, :], ko, ss_, ALU.mult)
                        v.tensor_tensor(rt3[:, :], ko, cc_, ALU.mult)
                        v.tensor_tensor(rt4[:, :], ke, ss_, ALU.mult)
                        v.tensor_tensor(ke, rt1[:, :], rt2[:, :], ALU.subtract)
                        v.tensor_tensor(ko, rt3[:, :], rt4[:, :], ALU.add)
                        b.mk(v, dv)
                        M.set(("rot", idx, tc), dv.n)
                    M.set(("rotcol", idx), dv.n)
                    M.set(("rotcol", idx - 1), dv.n)
                    M.set(("rotpair", l), dv.n)
                sq_cols = []
                if pos < 8:
                    if pos % 2 == 1:
                        sq_cols = [pos - 1, pos]
                else:
                    sq_cols = [pos]
                for pc in sq_cols:
                    pidx = m * NCH + pc
                    base = (m - 1) * T
                    for tc in range(TCH):
                        pe.wait(v._e, M.get(("ones", pidx, tc)))
                        if nct == 0:
                            plc.wait(v._e, M.get(("plc",)))
                        oct_ = M.get(("onesct", pidx, tc))
                        v.tensor_tensor(nrow[0:1, base + tc * 128: base + (tc + 1) * 128],
                                        nrow[0:1, base + tc * 128: base + (tc + 1) * 128],
                                        pss[oct_ % 2][0:1, 0:128], ALU.add)
                        b.mk(v, dv)
                        M.set(("sreldv", oct_), dv.n)
                        M.set(("srelac", oct_), 0)
                        nct += 1
        v.tensor_tensor(nrow[0:1, 2 * T:3 * T], nrow[0:1, 0:T], nrow[0:1, T:2 * T],
                        ALU.mult)
        b.mk(v, dv)
        M.set(("nw",), dv.n)
        for tc in range(TCH):
            pld[tc % 2].wait_k(v._e, 2 * (tc // 2 + 1))
            ac.wait(v._e, M.get(("wbce", tc)))
            wt = wbct[:, (tc % 2) * 128:(tc % 2) * 128 + 128]
            wct = M.get(("wbcct", tc))
            for oc in range(NCH):
                v.tensor_tensor(rt1[:, :],
                                kcol[:, (tc % 2) * NCH * 128 + oc * 128:
                                     (tc % 2) * NCH * 128 + (oc + 1) * 128],
                                vcol[:, (tc % 2) * NCH * 128 + oc * 128:
                                     (tc % 2) * NCH * 128 + (oc + 1) * 128],
                                ALU.mult)
                v.tensor_tensor(rt1[:, :], rt1[:, :], wt, ALU.mult)
                v.reduce_sum(rt2[:, 0:1], rt1[:, :], axis=AX.X)
                v.tensor_tensor(A_acc[:, oc:oc + 1], A_acc[:, oc:oc + 1],
                                rt2[:, 0:1], ALU.add)
            b.mk(v, dv)
            M.set(("p2", tc), dv.n)
            M.set(("sreldv", wct), dv.n)
            M.set(("srelac", wct), 0)

    def fix_ones_at():
        sqn = 0
        while ("sq_at", sqn) in M.d:
            pidx, tc = M.d[("sq_at", sqn)]
            M.d[("ones_at", sqn)] = M.d[("ones", pidx, tc)]
            sqn += 1

    with nc.Block() as blk:
        b.block = blk
        pe = b.sem("pe")
        dv = b.sem("dv")
        ac = b.sem("ac")
        plc = b.sem("plc")
        csem = b.dsem("csem")
        ldc = b.dsem("ldc")
        rst = b.dsem("rst")
        wls = [b.dsem("wl0"), b.dsem("wl1")]
        st4 = [b.dsem(f"st{q_}") for q_ in range(4)]
        pld = [b.dsem("pl0"), b.dsem("pl1")]
        ast = b.dsem("ast")
        csl = [b.dsem("cl0"), b.dsem("cl1")]

        @blk.gpsimd
        def _(g):
            gw = b.wrap(g, "pool")
            csem.issue(g.dma_start(out=wqb[:, :], in_=w_q[:, :]))
            csem.issue(g.dma_start(out=wkb[:, :], in_=w_k[:, :]))
            csem.issue(g.dma_start(out=wvb[:, :], in_=w_v[:, :]))
            gw.memset(onesc[:, :], 1.0)
            gw.memset(onesrf[:, :], 1.0)
            gw.memset(A_acc[:, :], 0.0)
            gw.memset(nrow[0:1, 0:2 * T], 0.0)
            b.mk(gw, plc)
        M.d[("plc",)] = plc.n

        es_ln = ExitStack()
        b.es = es_ln
        ln_done = emit_ln(b, x_in, xn4b, pe, dv, ac, keep_ab=(a_all, b_all),
                          prefix="ln5")
        es_ln.close()
        b.es = es
        M.d[("lnd",)] = ln_done

        es_t.close()
        es_p = ExitStack()
        b.es = es_p
        wcol = b.sb("wcol", [128, 2 * NCH * 128], BF16)
        stag = b.sb("stag", [128, 4 * T], F32)
        cosb = b.sb("cosb", [128, 2 * T], F32)
        sinb = b.sb("sinb", [128, 2 * T], F32)
        sq2 = b.sb("sq2", [128, 2 * 128], F32)
        rt1 = b.sb("rt1", [128, 128], F32)
        rt2 = b.sb("rt2", [128, 128], F32)
        rt3 = b.sb("rt3", [128, 128], F32)
        rt4 = b.sb("rt4", [128, 128], F32)
        kcol = b.sb("kcol", [128, 2 * NCH * 128], F32)
        vcol = b.sb("vcol", [128, 2 * NCH * 128], F32)
        wbct = b.sb("wbct", [128, 2 * 128], F32)
        b.es = es

        pe_p = Counter(None); pe_p.n = pe.n
        dv_p = Counter(None); dv_p.n = dv.n
        ac_p = Counter(None); ac_p.n = ac.n
        plc_p = Counter(None); plc_p.n = plc.n
        cp = (pe_p, dv_p, ac_p, plc_p, DSem(None), DSem(None), DSem(None),
              [DSem(None), DSem(None)],
              [DSem(None) for _ in range(4)],
              [DSem(None), DSem(None)], DSem(None),
              [DSem(None), DSem(None)])
        cp[4].n = csem.n
        mock = MockEng()
        mock.__dict__["_e"] = mock
        mock.__dict__["_c"] = Counter(None)
        for _pass in range(2):
            sec_pe(mock, cp)
            sec_dve(mock, cp)
            sec_act(mock, cp)
            sec_sync(mock, cp)
            fix_ones_at()
            if _pass == 0:
                pe_p.n = pe.n; dv_p.n = dv.n; ac_p.n = ac.n; plc_p.n = plc.n
                for dsm in (cp[5], cp[6], cp[10]):
                    dsm.n = 0
                for pair in (cp[7], cp[8], cp[9], cp[11]):
                    for d_ in pair:
                        d_.n = 0
                cp[4].n = csem.n
        M.plan = False

        cr = (pe, dv, ac, plc, csem, ldc, rst, wls, st4, pld, ast, csl)

        @blk.sync
        def _(s):
            sec_sync(s, cr)

        @blk.tensor
        def _(t):
            sec_pe(t, cr)

        @blk.vector
        def _(v):
            v = b.wrap(v, "dve")
            sec_dve(v, cr)

        @blk.scalar
        def _(sc):
            sc = b.wrap(sc, "act")
            sec_act(sc, cr)

        es_p.close()
        b.check()
    es.close()
    return nc


def build_attn1_tail():
    nc = bass.Bass("TRN2", target_bir_lowering=False, debug=False)
    x_in = nc.dram_tensor("x_in", [D, T], F32, kind="ExternalInput").ap()
    q5_d = nc.dram_tensor("q5", [NCH, TCH, 128, 128], F32, kind="ExternalInput").ap()
    ar_d = nc.dram_tensor("ar", [1, T], F32, kind="ExternalInput").ap()
    br_d = nc.dram_tensor("br", [1, T], F32, kind="ExternalInput").ap()
    am_i = nc.dram_tensor("amine", [128, NCH], F32, kind="ExternalInput").ap()
    ap_i = nc.dram_tensor("apart", [128, NCH], F32, kind="ExternalInput").ap()
    w_o = nc.dram_tensor("w_o", [D, D], F32, kind="ExternalInput").ap()
    bo_i = nc.dram_tensor("bo", [128, NCH], F32, kind="ExternalInput").ap()
    x_out = nc.dram_tensor("x_out", [D, T], F32, kind="ExternalOutput").ap()
    wob_d = nc.dram_tensor("wob_d", [D, D], BF16, kind="Internal").ap()

    es = ExitStack()
    b = B(nc, es)
    wob = b.sb("wob", [128, NCH * D], BF16)
    arow = b.sb("arow", [1, T], F32)
    brow = b.sb("brow", [1, T], F32)
    amb = b.sb("amb", [128, NCH], F32)
    apb = b.sb("apb", [128, NCH], F32)
    Apc = b.sb("Apc", [128, NCH], F32)
    boc = b.sb("boc", [128, NCH], F32)
    onesrf = b.sb("onesrf", [1, 128], F32)
    qcol = b.sb("qcol", [128, 2 * NCH * 128], F32)
    x4col = b.sb("x4col", [128, 2 * NCH * 128], F32)
    evcol = b.sb("evcol", [128, 2 * NCH * 128], F32)
    ocol = b.sb("ocol", [128, 2 * NCH * 128], F32)
    yb = b.sb("yb", [128, 2 * NCH * 128], BF16)
    tt_ = b.sb("tt_", [128, 128], F32)
    ps1 = b.ps("ps1", [128, 512], F32)
    ps2 = b.ps("ps2", [128, 512], F32)
    ps3 = b.ps("ps3", [128, 512], F32)
    ps4 = b.ps("ps4", [128, 512], F32)
    ps5 = b.ps("ps5", [128, 512], F32)
    ps6 = b.ps("ps6", [128, 512], F32)
    ps7 = b.ps("ps7", [128, 512], F32)
    ps8 = b.ps("ps8", [128, 512], F32)
    psc = [ps1, ps2, ps3, ps4]
    psA = [ps5, ps7]
    psB = [ps6, ps8]

    M = Marks()

    def sec_sync(s, c):
        (pe, dv, ac, plc, csem, ldc, wls, qcl, xcl, sto) = c
        ldc.issue(s.dma_start(out=arow[:, :], in_=ar_d[:, :]))
        ldc.issue(s.dma_start(out=brow[:, :], in_=br_d[:, :]))
        ldc.issue(s.dma_start(out=amb[:, :], in_=am_i[:, :]))
        ldc.issue(s.dma_start(out=apb[:, :], in_=ap_i[:, :]))
        ldc.issue(s.dma_start(out=boc[:, :], in_=bo_i[:, :]))
        csem.wait_k(s, 1)
        for cc in range(NCH):
            wls.issue(s.dma_start(out=wob[:, cc * D:(cc + 1) * D],
                                  in_=wob_d[cc * 128:(cc + 1) * 128, :]))
        def emit_sto(j):
            dv.wait(s, M.get(("ocol", j)))
            sto[j % 2].issue(s.dma_start(
                out=x_out.rearrange("(c p) t -> p c t", p=128)[:, :, j * 128:(j + 1) * 128],
                in_=ocol[:, (j % 2) * NCH * 128:(j % 2 + 1) * NCH * 128]
                .rearrange("p (c t) -> p c t", c=NCH)))

        for tc in range(TCH):
            if tc >= 2:
                emit_sto(tc - 2)
            qcl[tc % 2].issue(s.dma_start(
                out=qcol[:, (tc % 2) * NCH * 128:(tc % 2 + 1) * NCH * 128]
                .rearrange("p (c t) -> p c t", c=NCH),
                in_=q5_d[:, tc, :, :].rearrange("c p t -> p c t")))
            xcl[tc % 2].issue(s.dma_start(
                out=x4col[:, (tc % 2) * NCH * 128:(tc % 2 + 1) * NCH * 128]
                .rearrange("p (c t) -> p c t", c=NCH),
                in_=x_in.rearrange("(c p) t -> p c t", p=128)[:, :, tc * 128:(tc + 1) * 128]))
        emit_sto(TCH - 2)
        emit_sto(TCH - 1)
        sto[0].wait_k(s, sto[0].n)
        sto[1].wait_k(s, sto[1].n)

    def sec_pe(t, c):
        (pe, dv, ac, plc, csem, ldc, wls, qcl, xcl, sto) = c
        plc.wait(t, M.get(("plc",)))
        ldc.wait_k(t, 5)
        wls.wait_k(t, NCH)
        for tc in range(TCH):
            if tc >= 4:
                dv.wait(t, M.get(("ocol", tc - 4)))
            inst = t.matmul(psA[tc % 2][:, 0:128], onesrf[0:1, :],
                            arow[0:1, tc * 128:(tc + 1) * 128], start=True, stop=True)
            pe.inc(inst)
            _n16(t)
            inst = t.matmul(psB[tc % 2][:, 0:128], onesrf[0:1, :],
                            brow[0:1, tc * 128:(tc + 1) * 128], start=True, stop=True)
            pe.inc(inst)
            _n16(t)
            M.set(("abc", tc), pe.n)
            dv.wait(t, M.get(("y", tc)))
            for dc in range(NCH):
                j = tc * NCH + dc
                if j >= 4:
                    ac.wait(t, M.get(("ev", j - 4)))
                inst = None
                for cc in range(NCH):
                    inst = t.matmul(psc[j % 4][:, 0:128],
                                    wob[:, cc * D + dc * 128: cc * D + dc * 128 + 128],
                                    yb[:, (tc % 2) * NCH * 128 + cc * 128:
                                       (tc % 2) * NCH * 128 + (cc + 1) * 128],
                                    start=(cc == 0), stop=(cc == NCH - 1))
                pe.inc(inst)
                _n16(t)
                M.set(("mm", j), pe.n)

    def sec_act(sc, c):
        (pe, dv, ac, plc, csem, ldc, wls, qcl, xcl, sto) = c
        for tc in range(TCH):
            if tc >= 2:
                dv.wait(sc._e, M.get(("ocol", tc - 2)))
            for dc in range(NCH):
                j = tc * NCH + dc
                pe.wait(sc._e, M.get(("mm", j)))
                sc.activation(evcol[:, (tc % 2) * NCH * 128 + dc * 128:
                                    (tc % 2) * NCH * 128 + (dc + 1) * 128],
                              psc[j % 4][:, 0:128], AF.Identity, bias=boc[:, dc:dc + 1])
                b.mk(sc, ac)
                M.set(("ev", j), ac.n)

    def sec_dve(v, c):
        (pe, dv, ac, plc, csem, ldc, wls, qcl, xcl, sto) = c
        ldc.wait_k(v._e, 5)
        v.tensor_tensor(Apc[:, :], amb[:, :], apb[:, :], ALU.add)
        v.tensor_scalar(Apc[:, :], Apc[:, :], 0.5, 0.5, ALU.mult, ALU.add)
        b.mk(v, dv)
        M.set(("apd",), dv.n)
        for tc in range(TCH):
            qcl[tc % 2].wait_k(v._e, tc // 2 + 1)
            for oc in range(NCH):
                v.tensor_scalar(yb[:, (tc % 2) * NCH * 128 + oc * 128:
                                   (tc % 2) * NCH * 128 + (oc + 1) * 128],
                                qcol[:, (tc % 2) * NCH * 128 + oc * 128:
                                     (tc % 2) * NCH * 128 + (oc + 1) * 128],
                                Apc[:, oc:oc + 1], None, ALU.mult)
            b.mk(v, dv)
            M.set(("y", tc), dv.n)
            xcl[tc % 2].wait_k(v._e, tc // 2 + 1)
            pe.wait(v._e, M.get(("abc", tc)))
            if tc >= 2:
                sto[tc % 2].wait_k(v._e, tc // 2)
            for dc in range(NCH):
                j = tc * NCH + dc
                ac.wait(v._e, M.get(("ev", j)))
                xt = x4col[:, (tc % 2) * NCH * 128 + dc * 128:
                           (tc % 2) * NCH * 128 + (dc + 1) * 128]
                v.tensor_tensor(tt_[:, :], xt, psA[tc % 2][:, 0:128], ALU.mult)
                v.tensor_tensor(tt_[:, :], tt_[:, :], psB[tc % 2][:, 0:128], ALU.add)
                v.tensor_tensor(tt_[:, :], tt_[:, :],
                                evcol[:, (tc % 2) * NCH * 128 + dc * 128:
                                      (tc % 2) * NCH * 128 + (dc + 1) * 128], ALU.add)
                v.tensor_tensor(ocol[:, (tc % 2) * NCH * 128 + dc * 128:
                                     (tc % 2) * NCH * 128 + (dc + 1) * 128],
                                tt_[:, :], xt, ALU.add)
            b.mk(v, dv)
            M.set(("ocol", tc), dv.n)

    with nc.Block() as blk:
        b.block = blk
        pe = b.sem("pe")
        dv = b.sem("dv")
        ac = b.sem("ac")
        plc = b.sem("plc")
        csem = b.dsem("csem")
        ldc = b.dsem("ldc")
        wls = b.dsem("wls")
        qcl = [b.dsem("qc0"), b.dsem("qc1")]
        xcl = [b.dsem("xc0"), b.dsem("xc1")]
        sto = [b.dsem("st0"), b.dsem("st1")]

        @blk.gpsimd
        def _(g):
            gw = b.wrap(g, "pool")
            csem.issue(g.dma_start(out=wob_d[:, :], in_=w_o[:, :]))
            gw.memset(onesrf[:, :], 1.0)
            b.mk(gw, plc)
        M.d[("plc",)] = plc.n

        pe_p = Counter(None); pe_p.n = pe.n
        dv_p = Counter(None); dv_p.n = dv.n
        ac_p = Counter(None); ac_p.n = ac.n
        plc_p = Counter(None); plc_p.n = plc.n
        cp = (pe_p, dv_p, ac_p, plc_p, DSem(None), DSem(None), DSem(None),
              [DSem(None), DSem(None)], [DSem(None), DSem(None)],
              [DSem(None), DSem(None)])
        cp[4].n = csem.n
        mock = MockEng()
        mock.__dict__["_e"] = mock
        mock.__dict__["_c"] = Counter(None)
        for _pass in range(2):
            sec_pe(mock, cp)
            sec_dve(mock, cp)
            sec_act(mock, cp)
            sec_sync(mock, cp)
            if _pass == 0:
                pe_p.n = pe.n; dv_p.n = dv.n; ac_p.n = ac.n; plc_p.n = plc.n
                for dsm in (cp[5], cp[6]):
                    dsm.n = 0
                for pair in (cp[7], cp[8], cp[9]):
                    pair[0].n = 0; pair[1].n = 0
                cp[4].n = csem.n
        M.plan = False

        cr = (pe, dv, ac, plc, csem, ldc, wls, qcl, xcl, sto)

        @blk.sync
        def _(s):
            sec_sync(s, cr)

        @blk.tensor
        def _(t):
            sec_pe(t, cr)

        @blk.vector
        def _(v):
            v = b.wrap(v, "dve")
            sec_dve(v, cr)

        @blk.scalar
        def _(sc):
            sc = b.wrap(sc, "act")
            sec_act(sc, cr)

        b.check()
    es.close()
    return nc



def _consts():
    ident = np.eye(128).astype(ml_dtypes.bfloat16)
    h128 = _hadamard(128).astype(ml_dtypes.bfloat16)
    h2048 = _hadamard(2048).astype(ml_dtypes.bfloat16)
    kr64 = np.kron(_hadamard(64), np.eye(2)).astype(ml_dtypes.bfloat16)
    return dict(ident=ident, h128=h128, h2048=h2048, kr64=kr64)


def _attn1_perm():
    pi = np.empty(D, dtype=np.int64)
    for c in range(4):
        pi[c * 128:(c + 1) * 128] = 2 * np.arange(c * 128, (c + 1) * 128)
    for c in range(4, 8):
        pi[c * 128:(c + 1) * 128] = 2 * np.arange((c - 4) * 128, (c - 3) * 128) + 1
    pi[1024:] = np.arange(1024, 2048)
    return pi


def _rot_tables(qbase):
    inv_freq = (1.0 / 10000.0 ** (np.arange(0, 1024, 2) / 1024.0))
    pos = qbase + np.arange(T, dtype=np.float64)
    ang = inv_freq[:, None] * pos[None, :]
    cost = np.cos(ang).astype(np.float32).reshape(4, 128, T).transpose(1, 0, 2).reshape(128, ROTC * T)
    sint = np.sin(ang).astype(np.float32).reshape(4, 128, T).transpose(1, 0, 2).reshape(128, ROTC * T)
    return np.ascontiguousarray(cost), np.ascontiguousarray(sint)


_PROGS = {}


def _prog(name):
    if name not in _PROGS:
        _PROGS[name] = {"ffn0": lambda: build_ffn("l0"),
                        "ffn1": lambda: build_ffn("l1"),
                        "qkv": build_attn0_qkv,
                        "mix": build_attn0_mix,
                        "front": build_attn1_front,
                        "tail": build_attn1_tail}[name]()
    return _PROGS[name]


def _cvec(v, nch):
    return np.ascontiguousarray(np.asarray(v, dtype=np.float32).reshape(nch, 128).T)


def kernel(**inputs):
    ins = {k: np.asarray(v) for k, v in inputs.items()}
    x = ins["L0_attn_x"].astype(np.float32, copy=False)
    C = _consts()
    identf = np.eye(128, dtype=np.float32)
    NEG_M = -60000.0
    ncores = 8

    def run(name, in_maps):
        nc = _prog(name)
        r = run_bass_kernel_spmd(nc, in_maps, core_ids=list(range(ncores)))
        return r.results

    xs = [np.ascontiguousarray(x[c // 2, (c % 2) * T:(c % 2 + 1) * T, :].T)
          for c in range(ncores)]

    f0 = dict(ident=C["ident"], h128=C["h128"], kr64=C["kr64"], h2048=C["h2048"],
              w_up=ins["L0_ffn_W_up"].astype(np.float32, copy=False),
              w_gate=ins["L0_ffn_W_gate"].astype(np.float32, copy=False),
              w_down=ins["L0_ffn_W_down"].astype(np.float32, copy=False),
              su_ug=_cvec(ins["L0_ffn_SU_ug"], 16),
              su_dp=_cvec(ins["L0_ffn_SU_dp"], 64),
              wsc_up=_cvec(ins["L0_ffn_Wscale_ug"][:HID], 64),
              wsc_gt=_cvec(ins["L0_ffn_Wscale_ug"][HID:], 64),
              wsc_dp=_cvec(ins["L0_ffn_Wscale_dp"], 16))
    f1 = dict(ident=C["ident"], h128=C["h128"], kr64=C["kr64"],
              w_up=ins["L1_ffn_W3"].astype(np.float32, copy=False),
              w_gate=ins["L1_ffn_W1"].astype(np.float32, copy=False),
              w_down=ins["L1_ffn_W2"].astype(np.float32, copy=False))

    res = run("ffn0", [dict(x_in=xs[c], **f0) for c in range(ncores)])
    x1 = [np.asarray(res[c]["x_out"]) for c in range(ncores)]

    res = run("qkv", [dict(x_in=x1[c],
                           wq=ins["L0_attn_Wq"].astype(np.float32, copy=False),
                           wk=ins["L0_attn_Wk"].astype(np.float32, copy=False),
                           wv=ins["L0_attn_Wv"].astype(np.float32, copy=False))
                      for c in range(ncores)])
    q_t = [np.asarray(res[c]["q_t"]) for c in range(ncores)]
    k_t = [np.asarray(res[c]["k_t"]) for c in range(ncores)]
    v_t = [np.asarray(res[c]["v_t"]) for c in range(ncores)]
    in3 = []
    for c in range(ncores):
        bsel = c // 2
        qbase = (c % 2) * T
        k_all = np.concatenate([k_t[2 * bsel], k_t[2 * bsel + 1]], axis=1)
        v_all = np.concatenate([v_t[2 * bsel], v_t[2 * bsel + 1]], axis=0)
        jj = np.arange(MASKW)[None, :] - MASK_OFF
        pp = np.arange(128)[:, None] + qbase
        mask = np.where(jj <= pp, 0.0, NEG_M).astype(np.float32)
        in3.append(dict(x_in=x1[c], q_t=q_t[c], k_t=k_all, v_t=v_all, mask=mask,
                        ident=C["ident"], identf=identf,
                        wo=ins["L0_attn_Wo"].astype(np.float32, copy=False)))
    res = run("mix", in3)
    x2 = [np.asarray(res[c]["x_out"]) for c in range(ncores)]

    res = run("ffn0", [dict(x_in=x2[c], **f0) for c in range(ncores)])
    x3 = [np.asarray(res[c]["x_out"]) for c in range(ncores)]

    res = run("ffn1", [dict(x_in=x3[c], **f1) for c in range(ncores)])
    x4 = [np.asarray(res[c]["x_out"]) for c in range(ncores)]

    pi = _attn1_perm()
    W_in = ins["L1_attn_W_in"].astype(np.float32, copy=False)
    b_in = ins["L1_attn_b_in"].astype(np.float32, copy=False)
    a1 = dict(w_q=np.ascontiguousarray(W_in[:, 0:D]),
              w_k=np.ascontiguousarray(W_in[:, D + pi]),
              w_v=np.ascontiguousarray(W_in[:, 2 * D + pi]),
              gam=_cvec(ins["L1_attn_gamma"], 16),
              bet=_cvec(ins["L1_attn_beta"], 16),
              bq=_cvec(b_in[0:D], 16),
              bk=np.ascontiguousarray(b_in[D + pi].reshape(16, 128).T),
              bv=np.ascontiguousarray(b_in[2 * D + pi].reshape(16, 128).T))
    tabs = [_rot_tables(0), _rot_tables(T)]
    res = run("front", [dict(x_in=x4[c], cost=tabs[c % 2][0], sint=tabs[c % 2][1],
                             **a1) for c in range(ncores)])
    q5 = [np.asarray(res[c]["q5"]) for c in range(ncores)]
    ar = [np.asarray(res[c]["ar"]) for c in range(ncores)]
    br = [np.asarray(res[c]["br"]) for c in range(ncores)]
    anat = []
    for c in range(ncores):
        vec = np.asarray(res[c]["a_part"]).T.flatten()
        nat = np.empty(D, dtype=np.float32)
        nat[pi] = vec
        anat.append(np.ascontiguousarray(nat.reshape(16, 128).T))
    bo = _cvec(ins["L1_attn_b_out"], 16)
    res = run("tail", [dict(x_in=x4[c], q5=q5[c], ar=ar[c], br=br[c],
                            amine=anat[c], apart=anat[2 * (c // 2) + 1 - c % 2],
                            w_o=ins["L1_attn_W_out"].astype(np.float32, copy=False),
                            bo=bo) for c in range(ncores)])
    x5 = [np.asarray(res[c]["x_out"]) for c in range(ncores)]

    res = run("ffn1", [dict(x_in=x5[c], **f1) for c in range(ncores)])

    out = np.empty((4, 2 * T, D), dtype=np.float32)
    for c in range(ncores):
        out[c // 2, (c % 2) * T:(c % 2 + 1) * T, :] = np.asarray(res[c]["x_out"]).T
    return out
